# version 06
# baseline (speedup 1.0000x reference)
import contextlib
import numpy as np
import ml_dtypes
import concourse.bass as bass
import concourse.mybir as mybir
from concourse.bass_utils import run_bass_kernel_spmd

F32 = mybir.dt.float32
BF16 = mybir.dt.bfloat16
AF = mybir.ActivationFunctionType
ALU = mybir.AluOpType
BFDT = ml_dtypes.bfloat16

BATCH, S, D, H, HD, FF = 4, 4096, 1024, 16, 64, 4096
TOWN = 2048
NG, GH = 2, 8
NKT = S // 128
BT = 1024
EPS = 1e-5
N_CORES = 8
DEBUG = False

ENGS = ("tensor", "vector", "scalar", "sync", "gpsimd")


class Sem:
    def __init__(self, handle, name):
        self.h, self.n, self.name = handle, 0, name


class Ctx:
    def __init__(self, nc, sems):
        self.nc = nc
        self.q = {e: [] for e in ENGS}
        self.sems = sems

    def waits(self, eng, *marks):
        best = {}
        for m in marks:
            if m is None:
                continue
            s, v = m
            if v > best.get(s, -1):
                best[s] = v
        for s, v in best.items():
            self.q[eng].append(lambda e, h=s.h, v=v: e.wait_ge(h, v))

    def op(self, eng, fn, sem, inc=1):
        sem.n += inc
        self.q[eng].append(lambda e, fn=fn, h=sem.h, i=inc: fn(e).then_inc(h, i))
        return (sem, sem.n)

    def op_(self, eng, fn):
        self.q[eng].append(fn)

    def block_barrier(self):
        marks = [(s, s.n) for s in self.sems]
        for eng in ENGS:
            self.waits(eng, *marks)

    def run(self):
        with self.nc.Block() as blk:
            for eng in ENGS:
                fns = self.q[eng]
                if fns:
                    getattr(blk, eng)(lambda e, fns=tuple(fns): [f(e) for f in fns])
        self.q = {e: [] for e in ENGS}


def _emit_attn(ctx, st, TE, DV, AC, SD, GD, m_const):
    nc = ctx.nc
    ctx.block_barrier()
    m_wq = ctx.op("gpsimd", lambda e: e.dma_start(
        out=st.wq_sb[:, :].rearrange("p (k f) -> p k f", k=8),
        in_=st.wq_blk.ap().rearrange("k p a b -> p k (a b)")), GD, 16)
    m_wk = ctx.op("gpsimd", lambda e: e.dma_start(
        out=st.wk_sb[:, :].rearrange("p (k f) -> p k f", k=8),
        in_=st.wk_blk.ap().rearrange("k p a b -> p k (a b)")), GD, 16)
    m_wv = ctx.op("gpsimd", lambda e: e.dma_start(
        out=st.wv_sb[:, :].rearrange("p (k f) -> p k f", k=8),
        in_=st.wv_blk.ap().rearrange("k p a b -> p k (a b)")), GD, 16)
    m_hm = ctx.op("gpsimd", lambda e: e.dma_start(
        out=st.hm_sb[:, :], in_=st.hm_cat.ap()), GD, 16)
    m_sel = ctx.op("gpsimd", lambda e: e.dma_start(
        out=st.sel_sb[:, :], in_=st.hm_sel.ap()), GD, 16)
    m_pz = ctx.op("vector", lambda e: e.memset(st.P_sb[:, :], 0.0), DV)

    m_xT_used = [None, None]
    m_lt_used = None
    m_lrs_rd = None
    m_osb_rd = None
    m_attn_marks = []
    m_atst_wr = None

    for hs in range(4):
        with contextlib.ExitStack() as esp:
            pb_ps = esp.enter_context(nc.psum_tensor(f"pb_ps{hs}", [128, 4 * 512], F32))
            a_ps = [esp.enter_context(nc.psum_tensor(f"a_ps{hs}_{i}", [128, 512], F32))
                    for i in range(2)]
            o_ps = [esp.enter_context(nc.psum_tensor(f"o_ps{hs}_{i}", [128, 512], F32))
                    for i in range(2)]
            ctx.block_barrier()
            m_pb_ev = [None, None, None, None]
            m_a_last = [None, None]
            m_a_rd = [None, None]
            m_exp = [None] * 8
            m_P_rd = [None, None]
            m_bc_ev = [None, None]
            m_bc_done = [None] * 8
            m_prod_rd = [None, None]
            m_prod2_rd = None
            m_bcsb_rd = [None, None]
            m_bcsb_rd_dv = [None, None]
            m_o_ev = [None, None]
            for tt in range(2):
                t = hs * 2 + tt
                buf = t % 2
                ctx.waits("gpsimd", m_xT_used[buf])
                m_x = ctx.op("gpsimd", lambda e, t=t, buf=buf: e.dma_start(
                    out=st.xTb_sb[:, buf * 4096:(buf + 1) * 4096]
                        .rearrange("p (a t) -> p a t", a=8),
                    in_=st.xT_nb.ap()[:, t * 512:(t + 1) * 512]
                        .rearrange("(a p) t -> p a t", p=128)), GD, 16)
                ctx.waits("tensor", m_x, m_wq, m_wk, m_wv, m_hm, m_sel, m_const)
                last_mm = None
                pb = 0
                for w_sb, dst in ((st.wk_sb, st.kT_t), (st.wv_sb, st.vT_t),
                                  (st.wq_sb, st.qT_t)):
                    for ko in range(8):
                        ctx.waits("tensor", m_pb_ev[pb])
                        mm = None
                        for a in range(8):
                            f = lambda e, w_sb=w_sb, ko=ko, a=a, buf=buf, pb=pb: e.matmul(
                                pb_ps[:, pb * 512:(pb + 1) * 512],
                                w_sb[:, ko * 1024 + a * 128:ko * 1024 + (a + 1) * 128],
                                st.xTb_sb[:, buf * 4096 + a * 512:buf * 4096 + (a + 1) * 512],
                                start=(a == 0), stop=(a == 7), skip_group_check=True)
                            mm = ctx.op("tensor", f, TE) if a == 7 else ctx.op_("tensor", f)
                        last_mm = mm
                        ctx.waits("scalar", mm)
                        m_pb_ev[pb] = ctx.op("scalar", lambda e, dst=dst, ko=ko, pb=pb: e.copy(
                            dst[:, ko * 512:(ko + 1) * 512],
                            pb_ps[:, pb * 512:(pb + 1) * 512]), AC)
                        pb = 1 - pb
                m_xT_used[buf] = last_mm
                kqv_done = [m_pb_ev[0], m_pb_ev[1]]

                def emit_score(hl):
                    i = hl % 2
                    ctx.waits("vector", *kqv_done)
                    ctx.waits("vector", m_prod_rd[0])
                    m_prod = ctx.op("vector", lambda e, hl=hl: e.tensor_mul(
                        st.prod_sb[:, 0:3072].rearrange("p (c t) -> p c t", c=6),
                        st.kT_t[:, 0:3072].rearrange("p (c t) -> p c t", c=6),
                        bass.AP(st.qT_t, hl * 512, [[4096, 128], [0, 6], [1, 512]])), DV)
                    ctx.waits("gpsimd", *kqv_done)
                    ctx.waits("gpsimd", m_prod_rd[0])
                    m_prodp = None
                    for c in (6, 7):
                        m_prodp = ctx.op("gpsimd", lambda e, c=c, hl=hl: e.tensor_mul(
                            st.prod_sb[:, c * 512:(c + 1) * 512],
                            st.kT_t[:, c * 512:(c + 1) * 512],
                            st.qT_t[:, hl * 512:(hl + 1) * 512]), GD, 16)
                    ctx.waits("tensor", m_prod, m_prodp, m_a_rd[i])
                    mm = None
                    for c in range(8):
                        f = lambda e, c=c, i=i: e.matmul(
                            a_ps[i][0:16, :],
                            st.hm_sb[:, c * 16:(c + 1) * 16],
                            st.prod_sb[:, c * 512:(c + 1) * 512],
                            start=(c == 0), stop=(c == 7), skip_group_check=True)
                        mm = ctx.op("tensor", f, TE) if c == 7 else ctx.op_("tensor", f)
                    m_a_last[i] = mm
                    m_prod_rd[0] = mm
                    ctx.waits("scalar", mm, m_P_rd[i], m_pz)
                    m_exp[hl] = ctx.op("scalar", lambda e, i=i: e.activation(
                        st.P_sb[0:16, i * 512:(i + 1) * 512],
                        a_ps[i][0:16, :], AF.Exp, scale=0.125), AC)
                    ctx.waits("tensor", m_exp[hl])
                    m_di = ctx.op("tensor", lambda e, i=i: e.matmul(
                        a_ps[i][0:1, :],
                        st.hm_sb[:, 192:193],
                        st.P_sb[:, i * 512:(i + 1) * 512],
                        start=True, stop=True, skip_group_check=True), TE)
                    m_P_rd[i] = m_di
                    ctx.waits("vector", m_di, m_lt_used)
                    m_a_rd[i] = ctx.op("vector", lambda e, i=i, hl=hl: e.tensor_copy(
                        st.l_t[0:1, hl * 512:(hl + 1) * 512],
                        a_ps[i][0:1, :]), DV)

                def emit_sel(hl):
                    i = hl % 2
                    bb = hl % 2
                    nonlocal m_prod2_rd
                    mev = None
                    for c in range(8):
                        cb = c % 4
                        ctx.waits("tensor", m_pb_ev[cb], m_exp[hl])
                        m_bcmm = ctx.op("tensor", lambda e, c=c, cb=cb, i=i: e.matmul(
                            pb_ps[:, cb * 512:(cb + 1) * 512],
                            st.sel_sb[0:16, c * 128:(c + 1) * 128],
                            st.P_sb[0:16, i * 512:(i + 1) * 512],
                            start=True, stop=True, skip_group_check=True), TE)
                        m_P_rd[i] = m_bcmm
                        if c % 2 == 1:
                            ctx.waits("scalar", m_bcmm, m_bcsb_rd[bb], m_bcsb_rd_dv[bb])
                            mev = ctx.op("scalar", lambda e, c=c, cb=cb, bb=bb: e.copy(
                                st.bc_sb[:, bb * 4096 + (c - 1) * 512:bb * 4096 + (c + 1) * 512],
                                pb_ps[:, (cb - 1) * 512:(cb + 1) * 512]), AC)
                            m_pb_ev[cb - 1] = mev
                            m_pb_ev[cb] = mev
                    m_bc_done[hl] = mev

                def emit_pair(hl):
                    i = hl % 2
                    bb = hl % 2
                    nonlocal m_prod2_rd
                    ctx.waits("vector", m_bc_done[hl], m_prod2_rd)
                    m_p2 = ctx.op("vector", lambda e, bb=bb: e.tensor_mul(
                        st.prod2_sb[:, 0:3072],
                        st.bc_sb[:, bb * 4096:bb * 4096 + 3072],
                        st.vT_t[:, 0:3072]), DV)
                    ctx.waits("gpsimd", m_bc_done[hl], m_prod2_rd)
                    m_p2p = None
                    for c in (6, 7):
                        m_p2p = ctx.op("gpsimd", lambda e, c=c, bb=bb: e.tensor_mul(
                            st.prod2_sb[:, c * 512:(c + 1) * 512],
                            st.bc_sb[:, bb * 4096 + c * 512:bb * 4096 + (c + 1) * 512],
                            st.vT_t[:, c * 512:(c + 1) * 512]), GD, 16)
                    m_bcsb_rd[bb] = m_p2p
                    m_bcsb_rd_dv[bb] = m_p2
                    ctx.waits("tensor", m_p2, m_p2p, m_o_ev[i])
                    mm = None
                    for c in range(8):
                        f = lambda e, c=c, i=i: e.matmul(
                            o_ps[i][0:64, :],
                            st.hm_sb[:, 128:192],
                            st.prod2_sb[:, c * 512:(c + 1) * 512],
                            start=(c == 0), stop=(c == 7), skip_group_check=True)
                        mm = ctx.op("tensor", f, TE) if c == 7 else ctx.op_("tensor", f)
                    m_prod2_rd = mm
                    ctx.waits("vector", mm, m_osb_rd)
                    m_o_ev[i] = ctx.op("vector", lambda e, hl=hl, tt=tt, i=i: e.tensor_copy(
                        st.o_sb[0:64, hl * 1024 + tt * 512:hl * 1024 + (tt + 1) * 512],
                        o_ps[i][0:64, :]), DV)

                for hl in range(10):
                    if hl < 8:
                        emit_score(hl)
                    if 1 <= hl < 9:
                        emit_sel(hl - 1)
                    if hl >= 2:
                        emit_pair(hl - 2)
                l_done = [m for m in m_a_rd if m is not None]
                ctx.waits("gpsimd", *l_done)
                ctx.waits("gpsimd", m_lrs_rd)
                m_rs = None
                for hl in range(8):
                    m_rs = ctx.op("gpsimd", lambda e, hl=hl, tt=tt: e.dma_start(
                        out=st.lrs_sb[tt * 32:(tt + 1) * 32, hl * 16:(hl + 1) * 16],
                        in_=st.l_t[0:1, hl * 512:(hl + 1) * 512]
                            .rearrange("r (w j) -> r w j", j=16)), GD, 16)
                m_lt_used = m_rs
            o_done = [m for m in m_o_ev if m is not None]
            ctx.run()

        with contextlib.ExitStack() as esp:
            o_tr = esp.enter_context(nc.psum_tensor(f"o_tr{hs}", [128, 2 * 1024], BF16))
            ctx.block_barrier()
            m_rcp = ctx.op("vector", lambda e: e.reciprocal(
                st.rcp_sb[0:64, :], st.lrs_sb[0:64, :]), DV)
            m_lrs_rd = m_rcp
            m_tr_ev = [None, None]
            for hl in range(8):
                tb = hl % 2
                ctx.waits("tensor", m_tr_ev[tb], m_const)
                mt = None
                for j in range(16):
                    f = lambda e, hl=hl, j=j, tb=tb: e.transpose(
                        o_tr[0:64, tb * 1024 + j * 64:tb * 1024 + (j + 1) * 64],
                        st.o_sb[0:64, hl * 1024:(hl + 1) * 1024]
                            .rearrange("p (w g) -> p w g", g=16)[:, :, j:j + 1]
                            .rearrange("p w g -> p (w g)"),
                        st.id_sb[0:64, 0:64])
                    mt = ctx.op("tensor", f, TE) if j == 15 else ctx.op_("tensor", f)
                m_osb_rd = mt
                ctx.waits("vector", mt, m_rcp, m_atst_wr)
                me = None
                for j in range(16):
                    f = lambda e, hl=hl, j=j, tb=tb: e.tensor_scalar(
                        st.at_st[0:64, j * 64:(j + 1) * 64],
                        o_tr[0:64, tb * 1024 + j * 64:tb * 1024 + (j + 1) * 64],
                        st.rcp_sb[0:64, hl * 16 + j:hl * 16 + j + 1], None, ALU.mult)
                    me = ctx.op("vector", f, DV) if j == 15 else ctx.op_("vector", f)
                m_tr_ev[tb] = me
                row0 = hl * 256 + hs * 64
                ctx.waits("sync", me)
                m_atst_wr = ctx.op("sync", lambda e, row0=row0: e.dma_start(
                    out=st.attn_dram.ap()[row0:row0 + 64, :],
                    in_=st.at_st[0:64, :]), SD, 16)
                m_attn_marks.append(m_atst_wr)
            if DEBUG and hs == 3:
                ctx.op("sync", lambda e: e.dma_start(
                    out=st.dbg_l.ap()[0:1, 0:4096], in_=st.l_t[0:1, :]), SD, 16)
                ctx.op("sync", lambda e: e.dma_start(
                    out=st.dbg_lrs.ap()[0:64, :], in_=st.lrs_sb[0:64, :]), SD, 16)
                ctx.op("sync", lambda e: e.dma_start(
                    out=st.dbg_P.ap(), in_=st.P_sb[:, :]), SD, 16)
                ctx.op("sync", lambda e: e.dma_start(
                    out=st.dbg_o.ap()[:, 0:8192], in_=st.o_sb[:, :]), SD, 16)
            ctx.run()
    return m_attn_marks


def _ln_chain(ctx, st, cbase, y2_sb, out_sb, TE, DV, AC, m_sq_in, m_y_in=None):
    ctx.waits("vector", m_y_in)
    m1 = ctx.op("vector", lambda e: e.tensor_scalar_mul(
        st.stats[:, cbase + 2:cbase + 3], st.stats[:, cbase:cbase + 1], 1.0 / D), DV)
    ctx.waits("vector", m1)
    m2 = ctx.op("vector", lambda e: e.tensor_mul(
        st.stats[:, cbase + 3:cbase + 4], st.stats[:, cbase + 2:cbase + 3],
        st.stats[:, cbase + 2:cbase + 3]), DV)
    ctx.waits("vector", m2, m_sq_in)
    m3 = ctx.op("vector", lambda e: e.scalar_tensor_tensor(
        st.stats[:, cbase + 4:cbase + 5], st.stats[:, cbase + 1:cbase + 2], 1.0 / D,
        st.stats[:, cbase + 3:cbase + 4], ALU.mult, ALU.subtract), DV)
    ctx.waits("scalar", m3)
    m4 = ctx.op("scalar", lambda e: e.activation(
        st.stats[:, cbase + 5:cbase + 6], st.stats[:, cbase + 4:cbase + 5],
        AF.Ln, bias=st.eps_sb[:, 0:1]), AC)
    ctx.waits("scalar", m4)
    m5 = ctx.op("scalar", lambda e: e.activation(
        st.stats[:, cbase + 6:cbase + 7], st.stats[:, cbase + 5:cbase + 6],
        AF.Exp, scale=-0.5), AC)
    ctx.waits("vector", m5)
    m_out = ctx.op("vector", lambda e: e.tensor_scalar(
        out_sb, y2_sb[:, :],
        st.stats[:, cbase + 2:cbase + 3], st.stats[:, cbase + 6:cbase + 7],
        ALU.subtract, ALU.mult), DV)
    return m_out


class St:
    pass


def build_nc():
    nc = bass.Bass("TRN2", target_bir_lowering=False)
    st = St()

    st.xT_nb = nc.dram_tensor("xT_nb", [D, S], F32, kind="ExternalInput")
    st.x_own = nc.dram_tensor("x_own", [TOWN, D], F32, kind="ExternalInput")
    st.wq_blk = nc.dram_tensor("wq_blk", [8, 128, 8, 128], F32, kind="ExternalInput")
    st.wk_blk = nc.dram_tensor("wk_blk", [8, 128, 8, 128], F32, kind="ExternalInput")
    st.wv_blk = nc.dram_tensor("wv_blk", [8, 128, 8, 128], F32, kind="ExternalInput")
    st.hm_cat = nc.dram_tensor("hm_cat", [128, 194], BF16, kind="ExternalInput")
    st.hm_sel = nc.dram_tensor("hm_sel", [128, 1024], BF16, kind="ExternalInput")
    st.woT = nc.dram_tensor("woT", [D, D], F32, kind="ExternalInput")
    st.bo_bc = nc.dram_tensor("bo_bc", [128, D], F32, kind="ExternalInput")
    st.w1_blk = nc.dram_tensor("w1_blk", [32, 128, 8, 128], F32, kind="ExternalInput")
    st.w2_blk = nc.dram_tensor("w2_blk", [32, 128, 8, 128], F32, kind="ExternalInput")
    st.h1_blk = nc.dram_tensor("h1_blk", [32, 128, 32, 128], F32, kind="ExternalInput")
    st.h2_blk = nc.dram_tensor("h2_blk", [32, 128, 32, 128], F32, kind="ExternalInput")
    st.w3T = nc.dram_tensor("w3T", [FF, D], F32, kind="ExternalInput")
    st.s1_pp = nc.dram_tensor("s1_pp", [128, 32], F32, kind="ExternalInput")
    st.s2_pp = nc.dram_tensor("s2_pp", [128, 32], F32, kind="ExternalInput")
    st.sh1_pp = nc.dram_tensor("sh1_pp", [128, 32], F32, kind="ExternalInput")
    st.sh2_pp = nc.dram_tensor("sh2_pp", [128, 32], F32, kind="ExternalInput")
    st.s3_bc = nc.dram_tensor("s3_bc", [128, D], F32, kind="ExternalInput")
    st.ident = nc.dram_tensor("ident", [128, 192], BF16, kind="ExternalInput")
    st.out = nc.dram_tensor("out", [TOWN, D], F32, kind="ExternalOutput")
    st.attn_dram = nc.dram_tensor("attn_dram", [TOWN, D], BF16, kind="Internal")
    st.x1_dram = nc.dram_tensor("x1_dram", [TOWN, D], F32, kind="Internal")
    st.x1T_dram = nc.dram_tensor("x1T_dram", [8, 128, TOWN], BF16, kind="Internal")
    if DEBUG:
        st.dbg_attn = nc.dram_tensor("dbg_attn", [TOWN, D], BF16, kind="ExternalOutput")
        st.dbg_x1 = nc.dram_tensor("dbg_x1", [TOWN, D], F32, kind="ExternalOutput")
        st.dbg_h = nc.dram_tensor("dbg_h", [128, 32 * 1024], BF16, kind="ExternalOutput")
        st.dbg_l = nc.dram_tensor("dbg_l", [1, 4096], BF16, kind="ExternalOutput")
        st.dbg_lrs = nc.dram_tensor("dbg_lrs", [64, 128], BF16, kind="ExternalOutput")
        st.dbg_P = nc.dram_tensor("dbg_P", [128, 1024], BF16, kind="ExternalOutput")
        st.dbg_o = nc.dram_tensor("dbg_o", [128, 8192], BF16, kind="ExternalOutput")
        st.dbg_g1 = nc.dram_tensor("dbg_g1", [128, 32 * 1024], BF16, kind="ExternalOutput")

    with contextlib.ExitStack() as es0:
        TE = Sem(es0.enter_context(nc.semaphore("TE")), "TE")
        DV = Sem(es0.enter_context(nc.semaphore("DV")), "DV")
        AC = Sem(es0.enter_context(nc.semaphore("AC")), "AC")
        SD = Sem(es0.enter_context(nc.semaphore("SD")), "SD")
        GD = Sem(es0.enter_context(nc.semaphore("GD")), "GD")
        ctx = Ctx(nc, [TE, DV, AC, SD, GD])
        sb = lambda es, name, shape, dt: es.enter_context(nc.sbuf_tensor(name, shape, dt))

        st.bo_sb = sb(es0, "bo_sb", [128, D], F32)
        st.s3_sb = sb(es0, "s3_sb", [128, D], F32)
        st.id_sb = sb(es0, "id_sb", [128, 192], BF16)
        st.s1_sb = sb(es0, "s1_sb", [128, 32], F32)
        st.s2_sb = sb(es0, "s2_sb", [128, 32], F32)
        st.sh1_sb = sb(es0, "sh1_sb", [128, 32], F32)
        st.sh2_sb = sb(es0, "sh2_sb", [128, 32], F32)
        st.stats = sb(es0, "stats", [128, 16], F32)
        st.eps_sb = sb(es0, "eps_sb", [128, 1], F32)
        st.w12_sb = sb(es0, "w12_sb", [128, 4 * 1024], BF16)
        st.hb_sb = sb(es0, "hb_sb", [128, 2 * 4096], BF16)
        st.x1T_sb = sb(es0, "x1T_sb", [128, 8 * 1024], BF16)

        m_eps = ctx.op("vector", lambda e: e.memset(st.eps_sb[:, :], EPS), DV)
        m_const = None
        for dst, src in ((st.bo_sb, st.bo_bc), (st.s3_sb, st.s3_bc),
                         (st.id_sb, st.ident), (st.s1_sb, st.s1_pp),
                         (st.s2_sb, st.s2_pp), (st.sh1_sb, st.sh1_pp),
                         (st.sh2_sb, st.sh2_pp)):
            m_const = ctx.op("gpsimd",
                             lambda e, d=dst, s=src: e.dma_start(out=d[:, :], in_=s.ap()),
                             GD, 16)

        x1_wr_marks = []
        x1T_wr_marks = []
        wB = {}
        m_x1T_ld = [None]

        with contextlib.ExitStack() as esA:
            with contextlib.ExitStack() as es1:
                st.xTb_sb = sb(es1, "xTb_sb", [128, 2 * 4096], BF16)
                st.kT_t = sb(es1, "kT_t", [128, 8 * 512], BF16)
                st.vT_t = sb(es1, "vT_t", [128, 8 * 512], BF16)
                st.qT_t = sb(es1, "qT_t", [128, 8 * 512], BF16)
                st.o_sb = sb(es1, "o_sb", [128, 8 * 1024], BF16)
                st.P_sb = sb(es1, "P_sb", [128, 2 * 512], BF16)
                st.prod_sb = sb(es1, "prod_sb", [128, 4096], BF16)
                st.prod2_sb = sb(es1, "prod2_sb", [128, 4096], BF16)
                st.bc_sb = sb(es1, "bc_sb", [128, 2 * 4096], BF16)
                st.l_t = sb(es1, "l_t", [128, 4096], BF16)
                st.lrs_sb = sb(es1, "lrs_sb", [128, 128], BF16)
                st.rcp_sb = sb(es1, "rcp_sb", [128, 128], F32)
                st.wq_sb = sb(es1, "wq_sb", [128, 8 * 1024], BF16)
                st.wk_sb = sb(es1, "wk_sb", [128, 8 * 1024], BF16)
                st.wv_sb = sb(es1, "wv_sb", [128, 8 * 1024], BF16)
                st.hm_sb = sb(es1, "hm_sb", [128, 194], BF16)
                st.sel_sb = sb(es1, "sel_sb", [128, 1024], BF16)
                st.at_st = sb(es1, "at_st", [128, 1024], BF16)
                _emit_attn(ctx, st, TE, DV, AC, SD, GD, m_const)

            with contextlib.ExitStack() as es3:
                st.wo_sb = sb(es3, "wo_sb", [128, 8 * 1024], BF16)
                attnT_sb = sb(es3, "attnT_sb", [128, 2 * 1024], BF16)
                at_sb = sb(es3, "at_sb", [128, 2 * 1024], BF16)
                xo_sb = sb(es3, "xo_sb", [128, 2 * 1024], F32)
                y_sb = sb(es3, "y_sb", [128, 1024], F32)
                y2_sb = sb(es3, "y2_sb", [128, 1024], F32)
                sq_sb = sb(es3, "sq_sb", [128, 1024], F32)
                x1f_sb = sb(es3, "x1f_sb", [128, 2 * 1024], F32)
                x1b_sb = sb(es3, "x1b_sb", [128, 2 * 1024], BF16)
                x1T_st = sb(es3, "x1T_st", [128, 2 * 4096], BF16)
                ctx.block_barrier()
                if DEBUG:
                    ctx.op("sync", lambda e: e.dma_start(
                        out=st.dbg_attn.ap(), in_=st.attn_dram.ap()), SD, 16)
                m_wo = ctx.op("gpsimd", lambda e: e.dma_start(
                    out=st.wo_sb[:, :].rearrange("p (a t) -> p a t", a=8),
                    in_=st.woT.ap().rearrange("(a p) t -> p a t", p=128)), GD, 16)
                with contextlib.ExitStack() as esp:
                    tr_ps = esp.enter_context(nc.psum_tensor("tr_ps", [128, 1024], BF16))
                    trx_ps = esp.enter_context(nc.psum_tensor("trx_ps", [128, 1024], BF16))
                    wo_ps0 = esp.enter_context(nc.psum_tensor("wo_ps0", [128, 1024], F32))
                    wo_ps1 = esp.enter_context(nc.psum_tensor("wo_ps1", [128, 1024], F32))
                    wo_ps = (wo_ps0, wo_ps1)
                    m_tr_ev = None
                    m_trx_ev = None
                    m_at_used = [None, None]
                    m_x1b_used = [None, None]
                    m_bf = [None, None]
                    prev_womm = None
                    m_wo_rd = [None, None]
                    m_xo_used = [None, None]
                    m_x1_wr = [None, None]
                    m_x1T_wr = [None, None]
                    for qt in range(16):
                        buf = qt % 2
                        ctx.waits("sync", m_xo_used[buf])
                        m_xo = ctx.op("sync", lambda e, qt=qt, buf=buf: e.dma_start(
                            out=xo_sb[:, buf * 1024:(buf + 1) * 1024],
                            in_=st.x_own.ap()[qt * 128:(qt + 1) * 128, :]), SD, 16)
                        ctx.waits("sync", m_at_used[buf])
                        m_at = ctx.op("sync", lambda e, qt=qt, buf=buf: e.dma_start(
                            out=at_sb[:, buf * 1024:(buf + 1) * 1024],
                            in_=st.attn_dram.ap()[qt * 128:(qt + 1) * 128, :]), SD, 16)
                        ctx.waits("tensor", m_tr_ev, m_const, m_at)
                        mt = None
                        for a in range(8):
                            f = lambda e, qt=qt, a=a, buf=buf: e.transpose(
                                tr_ps[:, a * 128:(a + 1) * 128],
                                at_sb[:, buf * 1024 + a * 128:buf * 1024 + (a + 1) * 128],
                                st.id_sb[:, 0:128])
                            mt = ctx.op("tensor", f, TE) if a == 7 else ctx.op_("tensor", f)
                        m_at_used[buf] = mt
                        ctx.waits("vector", mt, prev_womm)
                        m_tr_ev = ctx.op("vector", lambda e, buf=buf: e.tensor_copy(
                            attnT_sb[:, buf * 1024:(buf + 1) * 1024], tr_ps[:, :]), DV)
                        ctx.waits("tensor", m_tr_ev, m_wo, m_wo_rd[buf])
                        mm = None
                        for half in range(2):
                            for a in range(8):
                                f = lambda e, half=half, a=a, buf=buf: e.matmul(
                                    wo_ps[buf][:, half * 512:(half + 1) * 512],
                                    attnT_sb[:, buf * 1024 + a * 128:buf * 1024 + (a + 1) * 128],
                                    st.wo_sb[:, a * 1024 + half * 512:a * 1024 + (half + 1) * 512],
                                    start=(a == 0), stop=(a == 7), skip_group_check=True)
                                last = (half == 1 and a == 7)
                                mm = ctx.op("tensor", f, TE) if last else ctx.op_("tensor", f)
                        prev_womm = mm
                        ctx.waits("vector", mm, m_xo, m_x1_wr[buf])
                        for half in range(2):
                            ctx.op_("vector", lambda e, half=half, buf=buf: e.tensor_add(
                                y_sb[:, half * 512:(half + 1) * 512],
                                wo_ps[buf][:, half * 512:(half + 1) * 512],
                                st.bo_sb[:, half * 512:(half + 1) * 512]))
                        m_y = ctx.op("vector", lambda e, buf=buf: e.scalar_tensor_tensor(
                            y2_sb[:, :], y_sb[:, :], 1.0,
                            xo_sb[:, buf * 1024:(buf + 1) * 1024],
                            ALU.mult, ALU.add, accum_out=st.stats[:, 0:1]), DV)
                        m_wo_rd[buf] = m_y
                        m_xo_used[buf] = m_y
                        ctx.waits("scalar", m_y)
                        m_sq = ctx.op("scalar", lambda e: e.activation(
                            sq_sb[:, :], y2_sb[:, :], AF.Square,
                            accum_out=st.stats[:, 1:2]), AC)
                        m_x1 = _ln_chain(ctx, st, 0, y2_sb,
                                         x1f_sb[:, buf * 1024:(buf + 1) * 1024],
                                         TE, DV, AC, m_sq, m_y)
                        ctx.waits("sync", m_x1)
                        m_x1_wr[buf] = ctx.op("sync", lambda e, qt=qt, buf=buf: e.dma_start(
                            out=st.x1_dram.ap()[qt * 128:(qt + 1) * 128, :],
                            in_=x1f_sb[:, buf * 1024:(buf + 1) * 1024]), SD, 16)
                        x1_wr_marks.append(m_x1_wr[buf])
                        def emit_trx(q2):
                            b2 = q2 % 2
                            nonlocal m_trx_ev
                            ctx.waits("tensor", m_bf[b2], m_trx_ev)
                            mtx = None
                            for a in range(8):
                                f = lambda e, a=a, b2=b2: e.transpose(
                                    trx_ps[:, a * 128:(a + 1) * 128],
                                    x1b_sb[:, b2 * 1024 + a * 128:b2 * 1024 + (a + 1) * 128],
                                    st.id_sb[:, 0:128])
                                mtx = ctx.op("tensor", f, TE) if a == 7 else ctx.op_("tensor", f)
                            m_x1b_used[b2] = mtx
                            sb4 = (q2 // 4) % 2
                            ctx.waits("vector", mtx,
                                      m_x1T_wr[sb4] if q2 % 4 == 0 else None)
                            m_trx_ev = ctx.op("vector", lambda e, q2=q2, sb4=sb4: e.tensor_copy(
                                x1T_st[:, sb4 * 4096:(sb4 + 1) * 4096]
                                    .rearrange("p (a t) -> p a t", a=8)
                                    [:, :, (q2 % 4) * 128:(q2 % 4 + 1) * 128],
                                trx_ps[:, :].rearrange("p (a t) -> p a t", a=8)), DV)
                            if q2 % 4 == 3:
                                ctx.waits("sync", m_trx_ev)
                                m_x1T_wr[sb4] = ctx.op("sync", lambda e, q2=q2, sb4=sb4: e.dma_start(
                                    out=st.x1T_dram.ap()[:, :, (q2 - 3) * 128:(q2 + 1) * 128]
                                        .rearrange("a p t -> p a t"),
                                    in_=x1T_st[:, sb4 * 4096:(sb4 + 1) * 4096]
                                        .rearrange("p (a t) -> p a t", a=8)), SD, 16)
                                x1T_wr_marks.append(m_x1T_wr[sb4])
                        if qt >= 2:
                            emit_trx(qt - 2)
                        ctx.waits("scalar", m_x1, m_x1b_used[buf])
                        m_bf[buf] = ctx.op("scalar", lambda e, buf=buf: e.copy(
                            x1b_sb[:, buf * 1024:(buf + 1) * 1024],
                            x1f_sb[:, buf * 1024:(buf + 1) * 1024]), AC)
                        if qt == 15:
                            emit_trx(14)
                            emit_trx(15)
                    for koo in range(2):
                        for li, wblk in ((0, st.w1_blk), (1, st.w2_blk)):
                            wB[(li, koo)] = ctx.op("gpsimd",
                                lambda e, wblk=wblk, li=li, koo=koo: e.dma_start(
                                    out=st.w12_sb[:, (li * 2 + koo % 2) * 1024:
                                                  (li * 2 + koo % 2 + 1) * 1024]
                                        .rearrange("p (a b) -> p a b", a=8),
                                    in_=wblk.ap()[koo]), GD, 16)
                    wB[("h", 0, 0)] = ctx.op("gpsimd", lambda e: e.dma_start(
                        out=st.hb_sb[:, 0:4096].rearrange("p (a b) -> p a b", a=32),
                        in_=st.h1_blk.ap()[0]), GD, 16)
                    ctx.waits("sync", *x1T_wr_marks)
                    m_x1T_ld[0] = ctx.op("sync", lambda e: e.dma_start(
                        out=st.x1T_sb[:, :].rearrange("p (a t) -> p a t", a=8),
                        in_=st.x1T_dram.ap()[:, :, 0:BT].rearrange("a p t -> p a t")), SD, 16)
                    ctx.run()

        with contextlib.ExitStack() as esB:
            ha_sb = sb(esB, "ha_sb", [128, 32 * 1024], BF16)
            silu_sb = sb(esB, "silu_sb", [128, 1024], F32)
            x1r_sb = sb(esB, "x1r_sb", [128, 2 * 1024], F32)
            yb_sb = sb(esB, "yb_sb", [128, 1024], F32)
            yb2_sb = sb(esB, "yb2_sb", [128, 1024], F32)
            x2_sb = sb(esB, "x2_sb", [128, 2 * 1024], F32)
            out_marks = []
            m_x1r = [None, None]
            for bt in range(2):
                with contextlib.ExitStack() as esg:
                    gb_sb = sb(esg, f"gb_sb{bt}", [128, 32 * 1024], BF16)
                    ctx.block_barrier()
                    if bt == 1:
                        m_x1T_ld[0] = ctx.op("sync", lambda e: e.dma_start(
                            out=st.x1T_sb[:, :].rearrange("p (a t) -> p a t", a=8),
                            in_=st.x1T_dram.ap()[:, :, BT:2 * BT]
                                .rearrange("a p t -> p a t")), SD, 16)
                        for koo in range(2):
                            for li, wblk in ((0, st.w1_blk), (1, st.w2_blk)):
                                wB[(li, koo)] = ctx.op("gpsimd",
                                    lambda e, wblk=wblk, li=li, koo=koo: e.dma_start(
                                        out=st.w12_sb[:, (li * 2 + koo % 2) * 1024:
                                                      (li * 2 + koo % 2 + 1) * 1024]
                                            .rearrange("p (a b) -> p a b", a=8),
                                        in_=wblk.ap()[koo]), GD, 16)
                        wB[("h", 0, 0)] = ctx.op("gpsimd", lambda e: e.dma_start(
                            out=st.hb_sb[:, 0:4096].rearrange("p (a b) -> p a b", a=32),
                            in_=st.h1_blk.ap()[0]), GD, 16)
                    with contextlib.ExitStack() as esp:
                        pp = tuple(esp.enter_context(nc.psum_tensor(f"pp{bt}_{i}", [128, 1024], F32))
                                   for i in range(4))
                        h_done = [None, None]
                        silu_used = None
                        w12_used = {0: [None, None], 1: [None, None]}
                        for ko in range(32):
                            par = ko % 2
                            if ko + 1 < 32:
                                for li, wblk in ((0, st.w1_blk), (1, st.w2_blk)):
                                    ctx.waits("gpsimd", w12_used[li][(ko + 1) % 2])
                                    wB[(li, ko + 1)] = ctx.op("gpsimd",
                                        lambda e, wblk=wblk, li=li, ko=ko: e.dma_start(
                                            out=st.w12_sb[:, (li * 2 + (ko + 1) % 2) * 1024:
                                                          (li * 2 + (ko + 1) % 2 + 1) * 1024]
                                                .rearrange("p (a b) -> p a b", a=8),
                                            in_=wblk.ap()[ko + 1]), GD, 16)
                            ctx.waits("tensor", wB[(0, ko)], wB[(1, ko)],
                                      m_x1T_ld[0], h_done[par])
                            mms = [None, None]
                            for li in range(2):
                                mm = None
                                for a in range(8):
                                    for hf in range(2):
                                        f = lambda e, li=li, a=a, hf=hf, par=par, ko=ko: e.matmul(
                                            pp[2 * par + li][:, hf * 512:(hf + 1) * 512],
                                            st.w12_sb[:, (li * 2 + ko % 2) * 1024 + a * 128:
                                                      (li * 2 + ko % 2) * 1024 + (a + 1) * 128],
                                            st.x1T_sb[:, a * 1024 + hf * 512:a * 1024 + (hf + 1) * 512],
                                            start=(a == 0), stop=(a == 7),
                                            skip_group_check=True)
                                        last = (a == 7 and hf == 1)
                                        mm = ctx.op("tensor", f, TE) if last else ctx.op_("tensor", f)
                                mms[li] = mm
                                w12_used[li][ko % 2] = mm
                            ctx.waits("scalar", mms[0], silu_used)
                            m_si = ctx.op("scalar", lambda e, par=par, ko=ko: e.activation(
                                silu_sb[:, :], pp[2 * par][:, :], AF.Silu,
                                scale=st.s1_sb[:, ko:ko + 1]), AC)
                            ctx.waits("vector", m_si, mms[1])
                            mh = None
                            for hf in range(2):
                                f = lambda e, par=par, ko=ko, hf=hf: e.scalar_tensor_tensor(
                                    ha_sb[:, ko * 1024 + hf * 512:ko * 1024 + (hf + 1) * 512],
                                    pp[2 * par + 1][:, hf * 512:(hf + 1) * 512],
                                    st.s2_sb[:, ko:ko + 1],
                                    silu_sb[:, hf * 512:(hf + 1) * 512],
                                    ALU.mult, ALU.mult)
                                mh = ctx.op("vector", f, DV) if hf == 1 else ctx.op_("vector", f)
                            h_done[par] = mh
                            silu_used = mh
                        t_done = [m for m in h_done if m is not None]
                        if DEBUG and bt == 0:
                            ctx.waits("sync", *t_done)
                            ctx.op("sync", lambda e: e.dma_start(
                                out=st.dbg_h.ap(), in_=ha_sb[:, :]), SD, 16)
                        hko_used = [None, None]
                        g1_last = None
                        g_done = [None, None]
                        for phase in range(2):
                            src = ha_sb if phase == 0 else gb_sb
                            dst = gb_sb if phase == 0 else ha_sb
                            ssb = st.sh1_sb if phase == 0 else st.sh2_sb
                            g_done = [None, None]
                            for ko in range(32):
                                gi = phase * 32 + ko
                                par = ko % 2
                                nxt = gi + 1
                                if nxt < 64:
                                    nblk = st.h1_blk if nxt < 32 else st.h2_blk
                                    nko = nxt % 32
                                    ctx.waits("gpsimd", hko_used[nxt % 2])
                                    wB[("h", nxt // 32, nko)] = ctx.op("gpsimd",
                                        lambda e, nblk=nblk, nko=nko, nxt=nxt: e.dma_start(
                                            out=st.hb_sb[:, (nxt % 2) * 4096:(nxt % 2 + 1) * 4096]
                                                .rearrange("p (a b) -> p a b", a=32),
                                            in_=nblk.ap()[nko]), GD, 16)
                                deps = [wB[("h", phase, ko)], g_done[par]]
                                if phase == 0 and ko == 0:
                                    deps += t_done
                                if phase == 1 and ko == 0:
                                    deps.append(g1_last)
                                ctx.waits("tensor", *deps)
                                mm = None
                                for a in range(32):
                                    for hf in range(2):
                                        f = lambda e, a=a, hf=hf, par=par, gi=gi, src=src: e.matmul(
                                            pp[par][:, hf * 512:(hf + 1) * 512],
                                            st.hb_sb[:, (gi % 2) * 4096 + a * 128:
                                                     (gi % 2) * 4096 + (a + 1) * 128],
                                            src[:, a * 1024 + hf * 512:a * 1024 + (hf + 1) * 512],
                                            start=(a == 0), stop=(a == 31),
                                            skip_group_check=True)
                                        last = (a == 31 and hf == 1)
                                        mm = ctx.op("tensor", f, TE) if last else ctx.op_("tensor", f)
                                hko_used[gi % 2] = mm
                                ctx.waits("scalar", mm)
                                g_done[par] = ctx.op("scalar",
                                    lambda e, par=par, ko=ko, dst=dst, ssb=ssb: e.activation(
                                        dst[:, ko * 1024:(ko + 1) * 1024],
                                        pp[par][:, :], AF.Gelu,
                                        scale=ssb[:, ko:ko + 1]), AC)
                            if phase == 0:
                                g1_last = g_done[1] if g_done[1] is not None else g_done[0]
                        g2_marks = [m for m in g_done if m is not None]
                        if DEBUG and bt == 0:
                            ctx.waits("sync", g1_last)
                            ctx.op("sync", lambda e: e.dma_start(
                                out=st.dbg_g1.ap(), in_=gb_sb[:, :]), SD, 16)
                        for q0 in range(2):
                            ctx.waits("sync", *x1_wr_marks)
                            ctx.waits("sync", m_x1r[q0])
                            m_x1r[q0] = ctx.op("sync", lambda e, bt=bt, q0=q0: e.dma_start(
                                out=x1r_sb[:, q0 * 1024:(q0 + 1) * 1024],
                                in_=st.x1_dram.ap()[bt * BT + q0 * 128:bt * BT + (q0 + 1) * 128, :]),
                                SD, 16)
                        ctx.run()

                with contextlib.ExitStack() as esw:
                    w3_sb = sb(esw, f"w3_sb{bt}", [128, 32 * 1024], BF16)
                    ctx.block_barrier()
                    m_w3 = ctx.op("gpsimd", lambda e: e.dma_start(
                        out=w3_sb[:, :].rearrange("p (a t) -> p a t", a=32),
                        in_=st.w3T.ap().rearrange("(a p) t -> p a t", p=128)), GD, 16)
                    with contextlib.ExitStack() as esp:
                        w3_ps = tuple(esp.enter_context(
                            nc.psum_tensor(f"w3_ps{bt}_{i}", [128, 1024], F32)) for i in range(2))
                        w3_rd = [None, None]
                        m_out_wr = [None, None]
                        m_sq_prev = None
                        for qt in range(8):
                            buf = qt % 2
                            ctx.waits("tensor", m_w3, w3_rd[buf])
                            if qt == 0:
                                ctx.waits("tensor", *g2_marks)
                            mm = None
                            for half in range(2):
                                for a in range(32):
                                    f = lambda e, half=half, a=a, buf=buf, qt=qt: e.matmul(
                                        w3_ps[buf][:, half * 512:(half + 1) * 512],
                                        ha_sb[:, a * 1024 + qt * 128:a * 1024 + (qt + 1) * 128],
                                        w3_sb[:, a * 1024 + half * 512:a * 1024 + (half + 1) * 512],
                                        start=(a == 0), stop=(a == 31),
                                        skip_group_check=True)
                                    last = (half == 1 and a == 31)
                                    mm = ctx.op("tensor", f, TE) if last else ctx.op_("tensor", f)
                            ctx.waits("vector", mm, m_x1r[buf], m_sq_prev)
                            for half in range(2):
                                ctx.op_("vector", lambda e, half=half, buf=buf: e.tensor_mul(
                                    yb_sb[:, half * 512:(half + 1) * 512],
                                    w3_ps[buf][:, half * 512:(half + 1) * 512],
                                    st.s3_sb[:, half * 512:(half + 1) * 512]))
                            m_y = ctx.op("vector", lambda e, buf=buf: e.scalar_tensor_tensor(
                                yb2_sb[:, :], yb_sb[:, :], 1.0,
                                x1r_sb[:, buf * 1024:(buf + 1) * 1024],
                                ALU.mult, ALU.add, accum_out=st.stats[:, 8:9]), DV)
                            w3_rd[buf] = m_y
                            ctx.waits("scalar", m_y)
                            m_sq = ctx.op("scalar", lambda e: e.activation(
                                yb_sb[:, :], yb2_sb[:, :], AF.Square,
                                accum_out=st.stats[:, 9:10]), AC)
                            m_sq_prev = m_sq
                            if qt + 2 < 8:
                                ctx.waits("sync", m_y)
                                m_x1r[buf] = ctx.op("sync",
                                    lambda e, bt=bt, qt=qt, buf=buf: e.dma_start(
                                        out=x1r_sb[:, buf * 1024:(buf + 1) * 1024],
                                        in_=st.x1_dram.ap()[bt * BT + (qt + 2) * 128:
                                                            bt * BT + (qt + 3) * 128, :]), SD, 16)
                            else:
                                m_x1r[buf] = None
                            ctx.waits("vector", m_out_wr[buf])
                            m_x2 = _ln_chain(ctx, st, 8, yb2_sb,
                                             x2_sb[:, buf * 1024:(buf + 1) * 1024],
                                             TE, DV, AC, m_sq, m_y)
                            ctx.waits("sync", m_x2)
                            m_out_wr[buf] = ctx.op("sync",
                                lambda e, bt=bt, qt=qt, buf=buf: e.dma_start(
                                    out=st.out.ap()[bt * BT + qt * 128:bt * BT + (qt + 1) * 128, :],
                                    in_=x2_sb[:, buf * 1024:(buf + 1) * 1024]), SD, 16)
                            out_marks.append(m_out_wr[buf])
                        if bt == 1:
                            if DEBUG:
                                ctx.op("sync", lambda e: e.dma_start(
                                    out=st.dbg_x1.ap(), in_=st.x1_dram.ap()), SD, 16)
                            ctx.waits("sync", out_marks[-1])
                        ctx.run()
    return nc


def _blkT(wT, ko_n, g_n):
    Din, Dout = wT.shape
    assert Din == g_n * 128 and Dout == ko_n * 128
    return np.ascontiguousarray(wT.reshape(g_n, 128, ko_n, 128).transpose(2, 1, 0, 3))


def kernel(L0_attn_x, L0_attn_Wq, L0_attn_Wk, L0_attn_Wv, L0_attn_Wo, L0_attn_bo,
           L0_ffn_w1_w, L0_ffn_w1_s, L0_ffn_w2_w, L0_ffn_w2_s,
           L0_ffn_h1_w, L0_ffn_h1_s, L0_ffn_h2_w, L0_ffn_h2_s,
           L0_ffn_w3_w, L0_ffn_w3_s):
    f32 = lambda a: np.ascontiguousarray(np.asarray(a, dtype=np.float32))
    x = f32(L0_attn_x)
    Wq, Wk, Wv, Wo = f32(L0_attn_Wq), f32(L0_attn_Wk), f32(L0_attn_Wv), f32(L0_attn_Wo)
    bo = f32(L0_attn_bo)
    w1, w2 = f32(L0_ffn_w1_w), f32(L0_ffn_w2_w)
    h1, h2 = f32(L0_ffn_h1_w), f32(L0_ffn_h2_w)
    w3 = f32(L0_ffn_w3_w)
    s1, s2 = f32(L0_ffn_w1_s), f32(L0_ffn_w2_s)
    sh1, sh2 = f32(L0_ffn_h1_s), f32(L0_ffn_h2_s)
    s3 = f32(L0_ffn_w3_s)

    pidx = np.arange(128)
    hm_cat = np.zeros((128, 194), np.float32)
    for ci in range(8):
        for hh in range(16):
            hm_cat[:, ci * 16 + hh] = (hh == 2 * ci + pidx // 64)
    for di in range(64):
        hm_cat[di, 128 + di] = 1.0
        hm_cat[64 + di, 128 + di] = 1.0
    hm_cat[0:16, 192] = 1.0
    hm_cat[64:80, 193] = 1.0
    hm_sel = np.zeros((128, 1024), np.float32)
    for ci in range(8):
        for hh in range(16):
            hm_sel[hh, ci * 128:(ci + 1) * 128] = (hh == 2 * ci + pidx // 64)
            hm_sel[64 + hh, ci * 128:(ci + 1) * 128] = (hh == 2 * ci + pidx // 64)

    shared = {
        "wk_blk": _blkT(Wk.T, 8, 8),
        "wv_blk": _blkT(Wv.T, 8, 8),
        "hm_cat": hm_cat.astype(BFDT),
        "hm_sel": hm_sel.astype(BFDT),
        "woT": np.ascontiguousarray(Wo.T),
        "bo_bc": np.ascontiguousarray(np.broadcast_to(bo, (128, D))),
        "w1_blk": _blkT(w1.T, 32, 8),
        "w2_blk": _blkT(w2.T, 32, 8),
        "h1_blk": _blkT(h1.T, 32, 32),
        "h2_blk": _blkT(h2.T, 32, 32),
        "w3T": np.ascontiguousarray(w3.T),
        "s1_pp": np.ascontiguousarray(s1.reshape(32, 128).T),
        "s2_pp": np.ascontiguousarray(s2.reshape(32, 128).T),
        "sh1_pp": np.ascontiguousarray(sh1.reshape(32, 128).T),
        "sh2_pp": np.ascontiguousarray(sh2.reshape(32, 128).T),
        "s3_bc": np.ascontiguousarray(np.broadcast_to(s3, (128, D))),
        "ident": np.concatenate(
            [np.eye(128, dtype=np.float32),
             np.concatenate([np.eye(64, dtype=np.float32)] * 2, axis=0)],
            axis=1).astype(BFDT),
    }
    xT = [np.ascontiguousarray(x[n].T) for n in range(BATCH)]
    WqT = Wq.T
    wq_half = []
    for h in range(2):
        wh = WqT[:, h * 512:(h + 1) * 512]
        dup = np.empty((D, 1024), np.float32)
        for hl in range(8):
            blk = wh[:, hl * 64:(hl + 1) * 64]
            dup[:, hl * 128:hl * 128 + 64] = blk
            dup[:, hl * 128 + 64:(hl + 1) * 128] = blk
        wq_half.append(_blkT(np.ascontiguousarray(dup), 8, 8))
    in_maps = []
    for c in range(N_CORES):
        n, half = c // 2, c % 2
        m = dict(shared)
        m["xT_nb"] = xT[n]
        m["wq_blk"] = wq_half[half]
        m["x_own"] = np.ascontiguousarray(x[n, half * TOWN:(half + 1) * TOWN, :])
        in_maps.append(m)

    nc = build_nc()
    res = run_bass_kernel_spmd(nc, in_maps, core_ids=list(range(N_CORES)))
    outv = np.empty((BATCH, S, D), np.float32)
    for c in range(N_CORES):
        n, half = c // 2, c % 2
        outv[n, half * TOWN:(half + 1) * TOWN, :] = res.results[c]["out"]
    return outv


# version 16
# speedup vs baseline: 1.1144x; 1.1144x over previous
import contextlib
import numpy as np
import ml_dtypes
import concourse.bass as bass
import concourse.mybir as mybir
from concourse.bass_utils import run_bass_kernel_spmd

F32 = mybir.dt.float32
BF16 = mybir.dt.bfloat16
AF = mybir.ActivationFunctionType
ALU = mybir.AluOpType
BFDT = ml_dtypes.bfloat16

BATCH, S, D, H, HD, FF = 4, 4096, 1024, 16, 64, 4096
TOWN = 2048
NG, GH = 2, 8
NKT = S // 128
BT = 1024
EPS = 1e-5
N_CORES = 8
DEBUG = False

ENGS = ("tensor", "vector", "scalar", "sync", "gpsimd")


class Sem:
    def __init__(self, handle, name):
        self.h, self.n, self.name = handle, 0, name


class Ctx:
    def __init__(self, nc, sems):
        self.nc = nc
        self.q = {e: [] for e in ENGS}
        self.sems = sems

    def waits(self, eng, *marks):
        best = {}
        for m in marks:
            if m is None:
                continue
            s, v = m
            if v > best.get(s, -1):
                best[s] = v
        for s, v in best.items():
            self.q[eng].append(lambda e, h=s.h, v=v: e.wait_ge(h, v))

    def waits_total(self, eng, *sems):
        for s in sems:
            if s.n > 0:
                self.q[eng].append(lambda e, h=s.h, v=s.n: e.wait_ge(h, v))

    def op(self, eng, fn, sem, inc=1):
        sem.n += inc
        self.q[eng].append(lambda e, fn=fn, h=sem.h, i=inc: fn(e).then_inc(h, i))
        return (sem, sem.n)

    def op_(self, eng, fn):
        self.q[eng].append(fn)

    def block_barrier(self):
        marks = [(s, s.n) for s in self.sems]
        for eng in ENGS:
            self.waits(eng, *marks)

    def run(self):
        with self.nc.Block() as blk:
            for eng in ENGS:
                fns = self.q[eng]
                if fns:
                    getattr(blk, eng)(lambda e, fns=tuple(fns): [f(e) for f in fns])
        self.q = {e: [] for e in ENGS}


def _emit_attn(ctx, st, TE, DV, AC, SD, GD, PL, m_const):
    nc = ctx.nc
    ctx.block_barrier()
    m_wq = ctx.op("gpsimd", lambda e: e.dma_start(
        out=st.wq_sb[:, :].rearrange("p (k f) -> p k f", k=8),
        in_=st.wq_blk.ap().rearrange("k p a b -> p k (a b)")), GD, 16)
    m_wk = ctx.op("gpsimd", lambda e: e.dma_start(
        out=st.wk_sb[:, :].rearrange("p (k f) -> p k f", k=8),
        in_=st.wk_blk.ap().rearrange("k p a b -> p k (a b)")), GD, 16)
    m_wv = ctx.op("gpsimd", lambda e: e.dma_start(
        out=st.wv_sb[:, :].rearrange("p (k f) -> p k f", k=8),
        in_=st.wv_blk.ap().rearrange("k p a b -> p k (a b)")), GD, 16)
    m_hm = ctx.op("gpsimd", lambda e: e.dma_start(
        out=st.hm_sb[:, :], in_=st.hm_cat.ap()), GD, 16)
    m_sel = ctx.op("gpsimd", lambda e: e.dma_start(
        out=st.sel_sb[:, :], in_=st.hm_sel.ap()), GD, 16)
    m_pz = ctx.op("vector", lambda e: e.memset(st.P_sb[:, :], 0.0), DV)

    m_xT_used = [None, None]
    m_xdma = [None] * 9
    m_lt_used = None
    m_lrs_rd = None
    m_osb_rd = None
    m_attn_marks = []
    m_atst_wr = None

    def emit_xdma(t):
        ctx.waits("gpsimd", m_xT_used[0])
        m_xdma[t] = ctx.op("gpsimd", lambda e, t=t: e.dma_start(
            out=st.xTb_sb[:, :].rearrange("p (a t) -> p a t", a=8),
            in_=st.xT_nb.ap()[:, t * 512:(t + 1) * 512]
                .rearrange("(a p) t -> p a t", p=128)), GD, 16)

    emit_xdma(0)
    m_trev_last = [None, None]
    m_pb_ev = [None, None, None, None]
    m_a_last = [None, None]
    m_a_rd = [None, None]
    m_exp = [None] * 8
    m_P_rd = [None, None]
    m_bc_done = None
    m_prod_rd = [None, None]
    m_prod_dv = [None, None]
    m_prod_gd = [None, None]
    m_prod2_rd = [None, None]
    m_p2_done = [None] * 8
    m_bcsb_rd = None
    m_bcsb_rd_dv = None
    m_o_ev = [None, None]
    for hs in range(4):
        with contextlib.ExitStack() as esp:
            pb_ps = esp.enter_context(nc.psum_tensor(f"pb_ps{hs}", [128, 4 * 512], F32))
            a_ps = [esp.enter_context(nc.psum_tensor(f"a_ps{hs}_{i}", [128, 512], F32))
                    for i in range(2)]
            o_ps = [esp.enter_context(nc.psum_tensor(f"o_ps{hs}_{i}", [128, 512], F32))
                    for i in range(2)]
            ctx.block_barrier()
            for tt in range(2):
                t = hs * 2 + tt
                buf = 0
                ctx.waits_total("tensor", GD)
                ctx.waits("scalar", m_prod_dv[0], m_prod_dv[1],
                          m_prod_gd[0], m_prod_gd[1], m_bcsb_rd, m_bcsb_rd_dv)
                last_mm = None
                pb = 0
                for w_sb, dst in ((st.wk_sb, st.kT_t), (st.wv_sb, st.vT_t),
                                  (st.wq_sb, st.qT_t)):
                    for ko in range(8):
                        ctx.waits("tensor", m_pb_ev[pb])
                        mm = None
                        for a in range(8):
                            f = lambda e, w_sb=w_sb, ko=ko, a=a, pb=pb: e.matmul(
                                pb_ps[:, pb * 512:(pb + 1) * 512],
                                w_sb[:, ko * 1024 + a * 128:ko * 1024 + (a + 1) * 128],
                                st.xTb_sb[:, a * 512:(a + 1) * 512],
                                start=(a == 0), stop=(a == 7), skip_group_check=True)
                            mm = ctx.op("tensor", f, TE) if a == 7 else ctx.op_("tensor", f)
                        last_mm = mm
                        ctx.waits("scalar", mm)
                        m_pb_ev[pb] = ctx.op("scalar", lambda e, dst=dst, ko=ko, pb=pb: e.copy(
                            dst[:, ko * 512:(ko + 1) * 512],
                            pb_ps[:, pb * 512:(pb + 1) * 512]), AC)
                        pb = 1 - pb
                m_xT_used[0] = last_mm
                if tt == 0:
                    emit_xdma(t + 1)
                kqv_done = [m_pb_ev[0], m_pb_ev[1]]

                def emit_prod(hl):
                    pp = hl % 2
                    ctx.waits("vector", *kqv_done)
                    ctx.waits("vector", m_prod_rd[pp])
                    m_prod_dv[pp] = ctx.op("vector", lambda e, hl=hl, pp=pp: e.tensor_mul(
                        st.prod_sb[:, pp * 4096:pp * 4096 + 3072]
                            .rearrange("p (c t) -> p c t", c=6),
                        st.kT_t[:, 0:3072].rearrange("p (c t) -> p c t", c=6),
                        bass.AP(st.qT_t, hl * 512, [[4096, 128], [0, 6], [1, 512]])), DV)
                    ctx.waits("gpsimd", *kqv_done)
                    ctx.waits("gpsimd", m_prod_rd[pp])
                    mp = None
                    for c in (6, 7):
                        mp = ctx.op("gpsimd", lambda e, c=c, hl=hl, pp=pp: e.tensor_mul(
                            st.prod_sb[:, pp * 4096 + c * 512:pp * 4096 + (c + 1) * 512],
                            st.kT_t[:, c * 512:(c + 1) * 512],
                            st.qT_t[:, hl * 512:(hl + 1) * 512]), PL)
                    m_prod_gd[pp] = mp

                def emit_score(hl):
                    i = hl % 2
                    pp = hl % 2
                    ctx.waits("tensor", m_prod_dv[pp], m_prod_gd[pp], m_a_rd[i])
                    mm = None
                    for c in range(8):
                        f = lambda e, c=c, pp=pp, i=i: e.matmul(
                            a_ps[i][0:16, :],
                            st.hm_sb[:, c * 16:(c + 1) * 16],
                            st.prod_sb[:, pp * 4096 + c * 512:pp * 4096 + (c + 1) * 512],
                            start=(c == 0), stop=(c == 7), skip_group_check=True)
                        mm = ctx.op("tensor", f, TE) if c == 7 else ctx.op_("tensor", f)
                    m_a_last[i] = mm
                    m_prod_rd[pp] = mm
                    ctx.waits("scalar", mm, m_P_rd[i], m_pz)
                    m_exp[hl] = ctx.op("scalar", lambda e, i=i: e.activation(
                        st.P_sb[0:16, i * 512:(i + 1) * 512],
                        a_ps[i][0:16, :], AF.Exp, scale=0.125), AC)
                    ctx.waits("tensor", m_exp[hl])
                    m_di = ctx.op("tensor", lambda e, i=i: e.matmul(
                        a_ps[i][0:1, :],
                        st.hm_sb[:, 192:193],
                        st.P_sb[:, i * 512:(i + 1) * 512],
                        start=True, stop=True, skip_group_check=True), TE)
                    m_P_rd[i] = m_di
                    ctx.waits("vector", m_di, m_lt_used)
                    m_a_rd[i] = ctx.op("vector", lambda e, i=i, hl=hl: e.tensor_copy(
                        st.l_t[0:1, hl * 512:(hl + 1) * 512],
                        a_ps[i][0:1, :]), DV)

                def emit_sel(hl):
                    i = hl % 2
                    nonlocal m_bc_done
                    mev = None
                    for c in range(8):
                        cb = c % 4
                        ctx.waits("tensor", m_pb_ev[cb], m_exp[hl])
                        m_bcmm = ctx.op("tensor", lambda e, c=c, cb=cb, i=i: e.matmul(
                            pb_ps[:, cb * 512:(cb + 1) * 512],
                            st.sel_sb[0:16, c * 128:(c + 1) * 128],
                            st.P_sb[0:16, i * 512:(i + 1) * 512],
                            start=True, stop=True, skip_group_check=True), TE)
                        m_P_rd[i] = m_bcmm
                        if c % 2 == 1:
                            ctx.waits("scalar", m_bcmm, m_bcsb_rd, m_bcsb_rd_dv)
                            mev = ctx.op("scalar", lambda e, c=c, cb=cb: e.copy(
                                st.bc_sb[:, (c - 1) * 512:(c + 1) * 512],
                                pb_ps[:, (cb - 1) * 512:(cb + 1) * 512]), AC)
                            m_pb_ev[cb - 1] = mev
                            m_pb_ev[cb] = mev
                    m_bc_done = mev

                def emit_prod2(hl):
                    p2 = hl % 2
                    nonlocal m_bcsb_rd, m_bcsb_rd_dv
                    ctx.waits("vector", m_bc_done, m_prod2_rd[p2])
                    m_p2 = ctx.op("vector", lambda e, p2=p2: e.tensor_mul(
                        st.prod2_sb[:, p2 * 4096:(p2 + 1) * 4096],
                        st.bc_sb[:, :],
                        st.vT_t[:, :]), DV)
                    m_bcsb_rd = m_p2
                    m_bcsb_rd_dv = m_p2
                    m_p2_done[hl] = (m_p2, m_p2)

                def emit_pair(hl):
                    i = hl % 2
                    p2 = hl % 2
                    ctx.waits("tensor", *m_p2_done[hl])
                    ctx.waits("tensor", m_o_ev[i])
                    mm = None
                    for c in range(8):
                        f = lambda e, c=c, i=i, p2=p2: e.matmul(
                            o_ps[i][0:64, :],
                            st.hm_sb[:, 128:192],
                            st.prod2_sb[:, p2 * 4096 + c * 512:p2 * 4096 + (c + 1) * 512],
                            start=(c == 0), stop=(c == 7), skip_group_check=True)
                        mm = ctx.op("tensor", f, TE) if c == 7 else ctx.op_("tensor", f)
                    m_prod2_rd[p2] = mm
                    ctx.waits("vector", mm, m_osb_rd)
                    m_o_ev[i] = ctx.op("vector", lambda e, hl=hl, tt=tt, i=i: e.tensor_copy(
                        st.o_sb[0:64, hl * 1024 + tt * 512:hl * 1024 + (tt + 1) * 512],
                        o_ps[i][0:64, :]), DV)

                emit_prod(0)
                for k in range(10):
                    if k + 1 < 8:
                        emit_prod(k + 1)
                    if k < 8:
                        emit_score(k)
                    if 1 <= k < 9:
                        emit_sel(k - 1)
                        emit_prod2(k - 1)
                    if k >= 2:
                        emit_pair(k - 2)
                l_done = [m for m in m_a_rd if m is not None]
                ctx.waits("gpsimd", *l_done)
                ctx.waits("gpsimd", m_lrs_rd)
                m_rs = None
                for hl in range(8):
                    m_rs = ctx.op("gpsimd", lambda e, hl=hl, tt=tt: e.dma_start(
                        out=st.lrs_sb[tt * 32:(tt + 1) * 32, hl * 16:(hl + 1) * 16],
                        in_=st.l_t[0:1, hl * 512:(hl + 1) * 512]
                            .rearrange("r (w j) -> r w j", j=16)), GD, 16)
                m_lt_used = m_rs
            o_done = [m for m in m_o_ev if m is not None]
            ctx.run()

        with contextlib.ExitStack() as esp:
            _dummy = esp.enter_context(nc.psum_tensor(f"dmy{hs}", [128, 6 * 512], F32))
            o_tr = esp.enter_context(nc.psum_tensor(f"o_tr{hs}", [128, 2 * 1024], BF16))
            ctx.block_barrier()
            if hs < 3:
                emit_xdma((hs + 1) * 2)
            ctx.waits("vector", m_lt_used)
            m_rcp = ctx.op("vector", lambda e: e.reciprocal(
                st.rcp_sb[0:64, :], st.lrs_sb[0:64, :]), DV)
            m_lrs_rd = m_rcp
            m_tr_ev = [None, None]
            first_tr_deps = [m_o_ev[0], m_o_ev[1], m_a_rd[0], m_a_rd[1],
                             m_pb_ev[0], m_pb_ev[1], m_pb_ev[2], m_pb_ev[3]]
            for hl in range(8):
                tb = hl % 2
                ctx.waits("tensor", m_tr_ev[tb], m_const)
                if hl == 0:
                    ctx.waits("tensor", *first_tr_deps)
                mt = None
                for j in range(16):
                    f = lambda e, hl=hl, j=j, tb=tb: e.transpose(
                        o_tr[0:64, tb * 1024 + j * 64:tb * 1024 + (j + 1) * 64],
                        st.o_sb[0:64, hl * 1024:(hl + 1) * 1024]
                            .rearrange("p (w g) -> p w g", g=16)[:, :, j:j + 1]
                            .rearrange("p w g -> p (w g)"),
                        st.id_sb[0:64, 0:64])
                    mt = ctx.op("tensor", f, TE) if j == 15 else ctx.op_("tensor", f)
                m_osb_rd = mt
                ctx.waits("vector", mt, m_rcp, m_atst_wr)
                me = None
                for j in range(16):
                    f = lambda e, hl=hl, j=j, tb=tb: e.tensor_scalar(
                        st.at_st[0:64, j * 64:(j + 1) * 64],
                        o_tr[0:64, tb * 1024 + j * 64:tb * 1024 + (j + 1) * 64],
                        st.rcp_sb[0:64, hl * 16 + j:hl * 16 + j + 1], None, ALU.mult)
                    me = ctx.op("vector", f, DV) if j == 15 else ctx.op_("vector", f)
                m_tr_ev[tb] = me
                m_trev_last[tb] = me
                m_o_ev[tb] = me
                row0 = hl * 256 + hs * 64
                ctx.waits("sync", me)
                m_atst_wr = ctx.op("sync", lambda e, row0=row0: e.dma_start(
                    out=st.attn_dram.ap()[row0:row0 + 64, :],
                    in_=st.at_st[0:64, :]), SD, 16)
                m_attn_marks.append(m_atst_wr)
            if DEBUG and hs == 3:
                ctx.op("sync", lambda e: e.dma_start(
                    out=st.dbg_l.ap()[0:1, 0:4096], in_=st.l_t[0:1, :]), SD, 16)
                ctx.op("sync", lambda e: e.dma_start(
                    out=st.dbg_lrs.ap()[0:64, :], in_=st.lrs_sb[0:64, :]), SD, 16)
                ctx.op("sync", lambda e: e.dma_start(
                    out=st.dbg_P.ap(), in_=st.P_sb[:, :]), SD, 16)
                ctx.op("sync", lambda e: e.dma_start(
                    out=st.dbg_o.ap()[:, 0:8192], in_=st.o_sb[:, :]), SD, 16)
            ctx.run()
    return m_attn_marks


def _ln_chain(ctx, st, cbase, y2_sb, out_sb, TE, DV, AC, m_sq_in, m_y_in=None):
    ctx.waits("vector", m_y_in)
    m1 = ctx.op("vector", lambda e: e.tensor_scalar_mul(
        st.stats[:, cbase + 2:cbase + 3], st.stats[:, cbase:cbase + 1], 1.0 / D), DV)
    ctx.waits("vector", m1)
    m2 = ctx.op("vector", lambda e: e.tensor_mul(
        st.stats[:, cbase + 3:cbase + 4], st.stats[:, cbase + 2:cbase + 3],
        st.stats[:, cbase + 2:cbase + 3]), DV)
    ctx.waits("vector", m2, m_sq_in)
    m3 = ctx.op("vector", lambda e: e.scalar_tensor_tensor(
        st.stats[:, cbase + 4:cbase + 5], st.stats[:, cbase + 1:cbase + 2], 1.0 / D,
        st.stats[:, cbase + 3:cbase + 4], ALU.mult, ALU.subtract), DV)
    ctx.waits("scalar", m3)
    m4 = ctx.op("scalar", lambda e: e.activation(
        st.stats[:, cbase + 5:cbase + 6], st.stats[:, cbase + 4:cbase + 5],
        AF.Ln, bias=st.eps_sb[:, 0:1]), AC)
    ctx.waits("scalar", m4)
    m5 = ctx.op("scalar", lambda e: e.activation(
        st.stats[:, cbase + 6:cbase + 7], st.stats[:, cbase + 5:cbase + 6],
        AF.Exp, scale=-0.5), AC)
    ctx.waits("vector", m5)
    m_out = ctx.op("vector", lambda e: e.tensor_scalar(
        out_sb, y2_sb[:, :],
        st.stats[:, cbase + 2:cbase + 3], st.stats[:, cbase + 6:cbase + 7],
        ALU.subtract, ALU.mult), DV)
    return m_out


class St:
    pass


def build_nc():
    nc = bass.Bass("TRN2", target_bir_lowering=False)
    st = St()

    st.xT_nb = nc.dram_tensor("xT_nb", [D, S], F32, kind="ExternalInput")
    st.x_own = nc.dram_tensor("x_own", [TOWN, D], F32, kind="ExternalInput")
    st.wq_blk = nc.dram_tensor("wq_blk", [8, 128, 8, 128], F32, kind="ExternalInput")
    st.wk_blk = nc.dram_tensor("wk_blk", [8, 128, 8, 128], F32, kind="ExternalInput")
    st.wv_blk = nc.dram_tensor("wv_blk", [8, 128, 8, 128], F32, kind="ExternalInput")
    st.hm_cat = nc.dram_tensor("hm_cat", [128, 194], BF16, kind="ExternalInput")
    st.hm_sel = nc.dram_tensor("hm_sel", [128, 1024], BF16, kind="ExternalInput")
    st.woT = nc.dram_tensor("woT", [D, D], F32, kind="ExternalInput")
    st.bo_bc = nc.dram_tensor("bo_bc", [128, D], F32, kind="ExternalInput")
    st.w1_blk = nc.dram_tensor("w1_blk", [32, 128, 8, 128], F32, kind="ExternalInput")
    st.w2_blk = nc.dram_tensor("w2_blk", [32, 128, 8, 128], F32, kind="ExternalInput")
    st.h1_blk = nc.dram_tensor("h1_blk", [32, 128, 32, 128], F32, kind="ExternalInput")
    st.h2_blk = nc.dram_tensor("h2_blk", [32, 128, 32, 128], F32, kind="ExternalInput")
    st.w3T = nc.dram_tensor("w3T", [FF, D], F32, kind="ExternalInput")
    st.s1_pp = nc.dram_tensor("s1_pp", [128, 32], F32, kind="ExternalInput")
    st.s2_pp = nc.dram_tensor("s2_pp", [128, 32], F32, kind="ExternalInput")
    st.sh1_pp = nc.dram_tensor("sh1_pp", [128, 32], F32, kind="ExternalInput")
    st.sh2_pp = nc.dram_tensor("sh2_pp", [128, 32], F32, kind="ExternalInput")
    st.s3_bc = nc.dram_tensor("s3_bc", [128, D], F32, kind="ExternalInput")
    st.ident = nc.dram_tensor("ident", [128, 192], BF16, kind="ExternalInput")
    st.out = nc.dram_tensor("out", [TOWN, D], F32, kind="ExternalOutput")
    st.attn_dram = nc.dram_tensor("attn_dram", [TOWN, D], BF16, kind="Internal")
    st.x1_dram = nc.dram_tensor("x1_dram", [TOWN, D], F32, kind="Internal")
    st.x1T_dram = nc.dram_tensor("x1T_dram", [8, 128, TOWN], BF16, kind="Internal")
    if DEBUG:
        st.dbg_attn = nc.dram_tensor("dbg_attn", [TOWN, D], BF16, kind="ExternalOutput")
        st.dbg_x1 = nc.dram_tensor("dbg_x1", [TOWN, D], F32, kind="ExternalOutput")
        st.dbg_h = nc.dram_tensor("dbg_h", [128, 32 * 1024], BF16, kind="ExternalOutput")
        st.dbg_l = nc.dram_tensor("dbg_l", [1, 4096], BF16, kind="ExternalOutput")
        st.dbg_lrs = nc.dram_tensor("dbg_lrs", [64, 128], BF16, kind="ExternalOutput")
        st.dbg_P = nc.dram_tensor("dbg_P", [128, 1024], BF16, kind="ExternalOutput")
        st.dbg_o = nc.dram_tensor("dbg_o", [128, 8192], BF16, kind="ExternalOutput")
        st.dbg_g1 = nc.dram_tensor("dbg_g1", [128, 32 * 1024], BF16, kind="ExternalOutput")

    with contextlib.ExitStack() as es0:
        TE = Sem(es0.enter_context(nc.semaphore("TE")), "TE")
        DV = Sem(es0.enter_context(nc.semaphore("DV")), "DV")
        AC = Sem(es0.enter_context(nc.semaphore("AC")), "AC")
        SD = Sem(es0.enter_context(nc.semaphore("SD")), "SD")
        GD = Sem(es0.enter_context(nc.semaphore("GD")), "GD")
        PL = Sem(es0.enter_context(nc.semaphore("PL")), "PL")
        ctx = Ctx(nc, [TE, DV, AC, SD, GD, PL])
        sb = lambda es, name, shape, dt: es.enter_context(nc.sbuf_tensor(name, shape, dt))

        st.bo_sb = sb(es0, "bo_sb", [128, D], F32)
        st.s3_sb = sb(es0, "s3_sb", [128, D], F32)
        st.id_sb = sb(es0, "id_sb", [128, 192], BF16)
        st.s1_sb = sb(es0, "s1_sb", [128, 32], F32)
        st.s2_sb = sb(es0, "s2_sb", [128, 32], F32)
        st.sh1_sb = sb(es0, "sh1_sb", [128, 32], F32)
        st.sh2_sb = sb(es0, "sh2_sb", [128, 32], F32)
        st.stats = sb(es0, "stats", [128, 16], F32)
        st.eps_sb = sb(es0, "eps_sb", [128, 1], F32)
        st.w12_sb = sb(es0, "w12_sb", [128, 4 * 1024], BF16)
        st.hb_sb = sb(es0, "hb_sb", [128, 2 * 4096], BF16)
        st.x1T_sb = sb(es0, "x1T_sb", [128, 8 * 1024], BF16)

        m_eps = ctx.op("vector", lambda e: e.memset(st.eps_sb[:, :], EPS), DV)
        m_const = None
        for dst, src in ((st.bo_sb, st.bo_bc), (st.s3_sb, st.s3_bc),
                         (st.id_sb, st.ident), (st.s1_sb, st.s1_pp),
                         (st.s2_sb, st.s2_pp), (st.sh1_sb, st.sh1_pp),
                         (st.sh2_sb, st.sh2_pp)):
            m_const = ctx.op("gpsimd",
                             lambda e, d=dst, s=src: e.dma_start(out=d[:, :], in_=s.ap()),
                             GD, 16)

        x1_wr_marks = []
        x1T_wr_marks = []
        wB = {}
        m_x1T_ld = [None]

        with contextlib.ExitStack() as esA:
            with contextlib.ExitStack() as es1:
                st.xTb_sb = sb(es1, "xTb_sb", [128, 4096], BF16)
                st.kT_t = sb(es1, "kT_t", [128, 8 * 512], BF16)
                st.vT_t = sb(es1, "vT_t", [128, 8 * 512], BF16)
                st.qT_t = sb(es1, "qT_t", [128, 8 * 512], BF16)
                st.o_sb = sb(es1, "o_sb", [128, 8 * 1024], BF16)
                st.P_sb = sb(es1, "P_sb", [128, 2 * 512], BF16)
                st.prod_sb = sb(es1, "prod_sb", [128, 2 * 4096], BF16)
                st.prod2_sb = sb(es1, "prod2_sb", [128, 2 * 4096], BF16)
                st.bc_sb = sb(es1, "bc_sb", [128, 4096], BF16)
                st.l_t = sb(es1, "l_t", [128, 4096], BF16)
                st.lrs_sb = sb(es1, "lrs_sb", [128, 128], BF16)
                st.rcp_sb = sb(es1, "rcp_sb", [128, 128], F32)
                st.wq_sb = sb(es1, "wq_sb", [128, 8 * 1024], BF16)
                st.wk_sb = sb(es1, "wk_sb", [128, 8 * 1024], BF16)
                st.wv_sb = sb(es1, "wv_sb", [128, 8 * 1024], BF16)
                st.hm_sb = sb(es1, "hm_sb", [128, 194], BF16)
                st.sel_sb = sb(es1, "sel_sb", [128, 1024], BF16)
                st.at_st = sb(es1, "at_st", [128, 1024], BF16)
                _emit_attn(ctx, st, TE, DV, AC, SD, GD, PL, m_const)

            with contextlib.ExitStack() as es3:
                st.wo_sb = sb(es3, "wo_sb", [128, 8 * 1024], BF16)
                attnT_sb = sb(es3, "attnT_sb", [128, 3 * 1024], BF16)
                at_sb = sb(es3, "at_sb", [128, 3 * 1024], BF16)
                xo_sb = sb(es3, "xo_sb", [128, 4 * 1024], F32)
                y_sb = sb(es3, "y_sb", [128, 1024], F32)
                y2_sb = sb(es3, "y2_sb", [128, 1024], F32)
                sq_sb = sb(es3, "sq_sb", [128, 1024], F32)
                x1f_sb = sb(es3, "x1f_sb", [128, 3 * 1024], F32)
                x1b_sb = sb(es3, "x1b_sb", [128, 2 * 1024], BF16)
                x1T_st = sb(es3, "x1T_st", [128, 2 * 4096], BF16)
                ctx.block_barrier()
                if DEBUG:
                    ctx.op("sync", lambda e: e.dma_start(
                        out=st.dbg_attn.ap(), in_=st.attn_dram.ap()), SD, 16)
                m_wo = ctx.op("gpsimd", lambda e: e.dma_start(
                    out=st.wo_sb[:, :].rearrange("p (a t) -> p a t", a=8),
                    in_=st.woT.ap().rearrange("(a p) t -> p a t", p=128)), GD, 16)
                with contextlib.ExitStack() as esp:
                    tr_ps = esp.enter_context(nc.psum_tensor("tr_ps", [128, 1024], BF16))
                    trx_ps = esp.enter_context(nc.psum_tensor("trx_ps", [128, 1024], BF16))
                    wo_ps = tuple(esp.enter_context(
                        nc.psum_tensor(f"wo_ps{i}", [128, 1024], F32)) for i in range(3))
                    m_tr_ev = None
                    m_trx_ev = None
                    m_at_used = [None, None, None]
                    m_x1b_used = [None, None]
                    m_bf = [None, None]
                    prev_womm = None
                    m_wo_rd = [None, None, None]
                    m_xo_used = [None, None, None, None]
                    m_x1_wr = [None, None, None]
                    m_x1T_wr = [None, None]
                    def emit_loads(qt):
                        buf = qt % 3
                        b4 = qt % 4
                        ctx.waits("gpsimd", m_xo_used[b4])
                        m_xo_l[qt] = ctx.op("gpsimd", lambda e, qt=qt, b4=b4: e.dma_start(
                            out=xo_sb[:, b4 * 1024:(b4 + 1) * 1024],
                            in_=st.x_own.ap()[qt * 128:(qt + 1) * 128, :]), GD, 16)
                        ctx.waits("gpsimd", m_at_used[buf])
                        m_at_l[qt] = ctx.op("gpsimd", lambda e, qt=qt, buf=buf: e.dma_start(
                            out=at_sb[:, buf * 1024:(buf + 1) * 1024],
                            in_=st.attn_dram.ap()[qt * 128:(qt + 1) * 128, :]), GD, 16)

                    def emit_tr_wo(qt):
                        nonlocal m_tr_ev, prev_womm
                        buf = qt % 3
                        ctx.waits("tensor", m_tr_ev, m_const)
                        ctx.waits_total("tensor", GD)
                        mt = None
                        for a in range(8):
                            f = lambda e, qt=qt, a=a, buf=buf: e.transpose(
                                tr_ps[:, a * 128:(a + 1) * 128],
                                at_sb[:, buf * 1024 + a * 128:buf * 1024 + (a + 1) * 128],
                                st.id_sb[:, 0:128])
                            mt = ctx.op("tensor", f, TE) if a == 7 else ctx.op_("tensor", f)
                        m_at_used[buf] = mt
                        ctx.waits("vector", mt, m_attnT_rd[buf])
                        m_tr_ev = ctx.op("vector", lambda e, buf=buf: e.tensor_copy(
                            attnT_sb[:, buf * 1024:(buf + 1) * 1024], tr_ps[:, :]), DV)
                        ctx.waits("tensor", m_tr_ev, m_wo, m_wo_rd[buf])
                        mm = None
                        for half in range(2):
                            for a in range(8):
                                f = lambda e, half=half, a=a, buf=buf: e.matmul(
                                    wo_ps[buf][:, half * 512:(half + 1) * 512],
                                    attnT_sb[:, buf * 1024 + a * 128:buf * 1024 + (a + 1) * 128],
                                    st.wo_sb[:, a * 1024 + half * 512:a * 1024 + (half + 1) * 512],
                                    start=(a == 0), stop=(a == 7), skip_group_check=True)
                                last = (half == 1 and a == 7)
                                mm = ctx.op("tensor", f, TE) if last else ctx.op_("tensor", f)
                        m_attnT_rd[buf] = mm
                        m_womm[qt] = mm
                        prev_womm = mm

                    def emit_ln(qt):
                        buf = qt % 3
                        ctx.waits("vector", m_womm[qt])
                        ctx.waits_total("vector", GD, SD)
                        m_ya = None
                        for half in range(2):
                            m_ya = ctx.op("vector", lambda e, half=half, buf=buf: e.tensor_add(
                                y_sb[:, half * 512:(half + 1) * 512],
                                wo_ps[buf][:, half * 512:(half + 1) * 512],
                                st.bo_sb[:, half * 512:(half + 1) * 512]), DV)
                        b4 = qt % 4
                        ctx.waits("vector", m_ya)
                        m_y = ctx.op("vector", lambda e, b4=b4: e.scalar_tensor_tensor(
                            y2_sb[:, :], y_sb[:, :], 1.0,
                            xo_sb[:, b4 * 1024:(b4 + 1) * 1024],
                            ALU.mult, ALU.add, accum_out=st.stats[:, 0:1]), DV)
                        m_wo_rd[buf] = m_y
                        m_xo_used[b4] = m_y
                        ctx.waits("scalar", m_y)
                        m_sq = ctx.op("scalar", lambda e: e.activation(
                            sq_sb[:, :], y2_sb[:, :], AF.Square,
                            accum_out=st.stats[:, 1:2]), AC)
                        m_x1 = _ln_chain(ctx, st, 0, y2_sb,
                                         x1f_sb[:, buf * 1024:(buf + 1) * 1024],
                                         TE, DV, AC, m_sq, m_y)
                        ctx.waits("sync", m_x1)
                        m_x1_wr[buf] = ctx.op("sync", lambda e, qt=qt, buf=buf: e.dma_start(
                            out=st.x1_dram.ap()[qt * 128:(qt + 1) * 128, :],
                            in_=x1f_sb[:, buf * 1024:(buf + 1) * 1024]), SD, 16)
                        x1_wr_marks.append(m_x1_wr[buf])
                        b2_ = qt % 2
                        ctx.waits("scalar", m_x1, m_x1b_used[b2_])
                        m_bf[b2_] = ctx.op("scalar", lambda e, buf=buf, b2_=b2_: e.copy(
                            x1b_sb[:, b2_ * 1024:(b2_ + 1) * 1024],
                            x1f_sb[:, buf * 1024:(buf + 1) * 1024]), AC)

                    def emit_trx(q2):
                        b2 = q2 % 2
                        nonlocal m_trx_ev
                        ctx.waits("tensor", m_bf[b2], m_trx_ev)
                        mtx = None
                        for a in range(8):
                            f = lambda e, a=a, b2=b2: e.transpose(
                                trx_ps[:, a * 128:(a + 1) * 128],
                                x1b_sb[:, b2 * 1024 + a * 128:b2 * 1024 + (a + 1) * 128],
                                st.id_sb[:, 0:128])
                            mtx = ctx.op("tensor", f, TE) if a == 7 else ctx.op_("tensor", f)
                        m_x1b_used[b2] = mtx
                        sb4 = (q2 // 4) % 2
                        ctx.waits("vector", mtx)
                        if q2 % 4 == 0 and m_x1T_wr[sb4] is not None:
                            ctx.waits_total("vector", SD)
                        m_trx_ev = ctx.op("vector", lambda e, q2=q2, sb4=sb4: e.tensor_copy(
                            x1T_st[:, sb4 * 4096:(sb4 + 1) * 4096]
                                .rearrange("p (a t) -> p a t", a=8)
                                [:, :, (q2 % 4) * 128:(q2 % 4 + 1) * 128],
                            trx_ps[:, :].rearrange("p (a t) -> p a t", a=8)), DV)
                        if q2 % 4 == 3:
                            ctx.waits("sync", m_trx_ev)
                            m_x1T_wr[sb4] = ctx.op("sync", lambda e, q2=q2, sb4=sb4: e.dma_start(
                                out=st.x1T_dram.ap()[:, :, (q2 - 3) * 128:(q2 + 1) * 128]
                                    .rearrange("a p t -> p a t"),
                                in_=x1T_st[:, sb4 * 4096:(sb4 + 1) * 4096]
                                    .rearrange("p (a t) -> p a t", a=8)), SD, 16)
                            x1T_wr_marks.append(m_x1T_wr[sb4])

                    m_xo_l = [None] * 16
                    m_at_l = [None] * 16
                    m_womm = [None] * 16
                    m_attnT_rd = [None, None, None]
                    emit_loads(0)
                    emit_loads(1)
                    for qt in range(16):
                        emit_tr_wo(qt)
                        if qt >= 3:
                            emit_trx(qt - 3)
                        if qt >= 1:
                            emit_ln(qt - 1)
                        if qt + 2 < 16:
                            emit_loads(qt + 2)
                    emit_trx(13)
                    emit_ln(15)
                    emit_trx(14)
                    emit_trx(15)
                    for koo in range(2):
                        for li, wblk in ((0, st.w1_blk), (1, st.w2_blk)):
                            wB[(li, koo)] = ctx.op("gpsimd",
                                lambda e, wblk=wblk, li=li, koo=koo: e.dma_start(
                                    out=st.w12_sb[:, (li * 2 + koo % 2) * 1024:
                                                  (li * 2 + koo % 2 + 1) * 1024]
                                        .rearrange("p (a b) -> p a b", a=8),
                                    in_=wblk.ap()[koo]), GD, 16)
                    wB[("h", 0, 0)] = ctx.op("gpsimd", lambda e: e.dma_start(
                        out=st.hb_sb[:, 0:4096].rearrange("p (a b) -> p a b", a=32),
                        in_=st.h1_blk.ap()[0]), GD, 16)
                    ctx.waits("sync", *x1T_wr_marks)
                    m_x1T_ld[0] = ctx.op("sync", lambda e: e.dma_start(
                        out=st.x1T_sb[:, :].rearrange("p (a t) -> p a t", a=8),
                        in_=st.x1T_dram.ap()[:, :, 0:BT].rearrange("a p t -> p a t")), SD, 16)
                    ctx.run()

        with contextlib.ExitStack() as esB:
            ha_sb = sb(esB, "ha_sb", [128, 32 * 1024], BF16)
            silu_sb = sb(esB, "silu_sb", [128, 1024], F32)
            x1r_sb = sb(esB, "x1r_sb", [128, 2 * 1024], F32)
            yb_sb = sb(esB, "yb_sb", [128, 1024], F32)
            yb2_sb = sb(esB, "yb2_sb", [128, 1024], F32)
            x2_sb = sb(esB, "x2_sb", [128, 2 * 1024], F32)
            out_marks = []
            m_x1r = [None, None]
            for bt in range(2):
                with contextlib.ExitStack() as esg:
                    gb_sb = sb(esg, f"gb_sb{bt}", [128, 32 * 1024], BF16)
                    ctx.block_barrier()
                    with contextlib.ExitStack() as esp:
                        pp = tuple(esp.enter_context(nc.psum_tensor(f"pp{bt}_{i}", [128, 1024], F32))
                                   for i in range(4))
                        h_done = [None, None]
                        silu_used = None
                        w12_used = {0: [None, None], 1: [None, None]}
                        for ko in range(32):
                            par = ko % 2
                            ctx.waits("tensor", wB[(0, ko)], wB[(1, ko)],
                                      m_x1T_ld[0], h_done[par])
                            if ko + 1 < 32:
                                for li, wblk in ((0, st.w1_blk), (1, st.w2_blk)):
                                    ctx.waits("gpsimd", w12_used[li][(ko + 1) % 2])
                                    wB[(li, ko + 1)] = ctx.op("gpsimd",
                                        lambda e, wblk=wblk, li=li, ko=ko: e.dma_start(
                                            out=st.w12_sb[:, (li * 2 + (ko + 1) % 2) * 1024:
                                                          (li * 2 + (ko + 1) % 2 + 1) * 1024]
                                                .rearrange("p (a b) -> p a b", a=8),
                                            in_=wblk.ap()[ko + 1]), GD, 16)
                            mms = [None, None]
                            for li in range(2):
                                mm = None
                                for a in range(8):
                                    for hf in range(2):
                                        f = lambda e, li=li, a=a, hf=hf, par=par, ko=ko: e.matmul(
                                            pp[2 * par + li][:, hf * 512:(hf + 1) * 512],
                                            st.w12_sb[:, (li * 2 + ko % 2) * 1024 + a * 128:
                                                      (li * 2 + ko % 2) * 1024 + (a + 1) * 128],
                                            st.x1T_sb[:, a * 1024 + hf * 512:a * 1024 + (hf + 1) * 512],
                                            start=(a == 0), stop=(a == 7),
                                            skip_group_check=True)
                                        last = (a == 7 and hf == 1)
                                        mm = ctx.op("tensor", f, TE) if last else ctx.op_("tensor", f)
                                mms[li] = mm
                                w12_used[li][ko % 2] = mm
                            ctx.waits("scalar", mms[0], silu_used)
                            m_si = ctx.op("scalar", lambda e, par=par, ko=ko: e.activation(
                                silu_sb[:, :], pp[2 * par][:, :], AF.Silu,
                                scale=st.s1_sb[:, ko:ko + 1]), AC)
                            ctx.waits("vector", m_si, mms[1])
                            mh = None
                            for hf in range(2):
                                f = lambda e, par=par, ko=ko, hf=hf: e.scalar_tensor_tensor(
                                    ha_sb[:, ko * 1024 + hf * 512:ko * 1024 + (hf + 1) * 512],
                                    pp[2 * par + 1][:, hf * 512:(hf + 1) * 512],
                                    st.s2_sb[:, ko:ko + 1],
                                    silu_sb[:, hf * 512:(hf + 1) * 512],
                                    ALU.mult, ALU.mult)
                                mh = ctx.op("vector", f, DV) if hf == 1 else ctx.op_("vector", f)
                            h_done[par] = mh
                            silu_used = mh
                        t_done = [m for m in h_done if m is not None]
                        if bt == 0:
                            ctx.waits("sync", *t_done)
                            m_x1T_ld[0] = ctx.op("sync", lambda e: e.dma_start(
                                out=st.x1T_sb[:, :].rearrange("p (a t) -> p a t", a=8),
                                in_=st.x1T_dram.ap()[:, :, BT:2 * BT]
                                    .rearrange("a p t -> p a t")), SD, 16)
                            for koo in range(2):
                                for li, wblk in ((0, st.w1_blk), (1, st.w2_blk)):
                                    ctx.waits("gpsimd", w12_used[li][koo % 2])
                                    wB[(li, koo)] = ctx.op("gpsimd",
                                        lambda e, wblk=wblk, li=li, koo=koo: e.dma_start(
                                            out=st.w12_sb[:, (li * 2 + koo % 2) * 1024:
                                                          (li * 2 + koo % 2 + 1) * 1024]
                                                .rearrange("p (a b) -> p a b", a=8),
                                            in_=wblk.ap()[koo]), GD, 16)
                        if DEBUG and bt == 0:
                            ctx.waits("sync", *t_done)
                            ctx.op("sync", lambda e: e.dma_start(
                                out=st.dbg_h.ap(), in_=ha_sb[:, :]), SD, 16)
                        hko_used = [None, None]
                        g1_last = None
                        g_done = [None, None]
                        for phase in range(2):
                            src = ha_sb if phase == 0 else gb_sb
                            dst = gb_sb if phase == 0 else ha_sb
                            ssb = st.sh1_sb if phase == 0 else st.sh2_sb
                            g_done = [None, None]
                            for ko in range(32):
                                gi = phase * 32 + ko
                                par = ko % 2
                                nxt = gi + 1
                                deps = [wB[("h", phase, ko)], g_done[par]]
                                if phase == 0 and ko == 0:
                                    deps += t_done
                                if phase == 1 and ko == 0:
                                    deps.append(g1_last)
                                ctx.waits("tensor", *deps)
                                if nxt < 64:
                                    nblk = st.h1_blk if nxt < 32 else st.h2_blk
                                    nko = nxt % 32
                                    ctx.waits("gpsimd", hko_used[nxt % 2])
                                    wB[("h", nxt // 32, nko)] = ctx.op("gpsimd",
                                        lambda e, nblk=nblk, nko=nko, nxt=nxt: e.dma_start(
                                            out=st.hb_sb[:, (nxt % 2) * 4096:(nxt % 2 + 1) * 4096]
                                                .rearrange("p (a b) -> p a b", a=32),
                                            in_=nblk.ap()[nko]), GD, 16)
                                mm = None
                                for a in range(32):
                                    for hf in range(2):
                                        f = lambda e, a=a, hf=hf, par=par, gi=gi, src=src: e.matmul(
                                            pp[par][:, hf * 512:(hf + 1) * 512],
                                            st.hb_sb[:, (gi % 2) * 4096 + a * 128:
                                                     (gi % 2) * 4096 + (a + 1) * 128],
                                            src[:, a * 1024 + hf * 512:a * 1024 + (hf + 1) * 512],
                                            start=(a == 0), stop=(a == 31),
                                            skip_group_check=True)
                                        last = (a == 31 and hf == 1)
                                        mm = ctx.op("tensor", f, TE) if last else ctx.op_("tensor", f)
                                hko_used[gi % 2] = mm
                                ctx.waits("scalar", mm)
                                g_done[par] = ctx.op("scalar",
                                    lambda e, par=par, ko=ko, dst=dst, ssb=ssb: e.activation(
                                        dst[:, ko * 1024:(ko + 1) * 1024],
                                        pp[par][:, :], AF.Gelu,
                                        scale=ssb[:, ko:ko + 1]), AC)
                            if phase == 0:
                                g1_last = g_done[1] if g_done[1] is not None else g_done[0]
                        g2_marks = [m for m in g_done if m is not None]
                        if bt == 0:
                            ctx.waits("gpsimd", hko_used[0])
                            wB[("h", 0, 0)] = ctx.op("gpsimd", lambda e: e.dma_start(
                                out=st.hb_sb[:, 0:4096].rearrange("p (a b) -> p a b", a=32),
                                in_=st.h1_blk.ap()[0]), GD, 16)
                        if DEBUG and bt == 0:
                            ctx.waits("sync", g1_last)
                            ctx.op("sync", lambda e: e.dma_start(
                                out=st.dbg_g1.ap(), in_=gb_sb[:, :]), SD, 16)
                        for q0 in range(2):
                            ctx.waits("sync", *x1_wr_marks)
                            ctx.waits("sync", m_x1r[q0])
                            m_x1r[q0] = ctx.op("sync", lambda e, bt=bt, q0=q0: e.dma_start(
                                out=x1r_sb[:, q0 * 1024:(q0 + 1) * 1024],
                                in_=st.x1_dram.ap()[bt * BT + q0 * 128:bt * BT + (q0 + 1) * 128, :]),
                                SD, 16)
                        ctx.run()

                with contextlib.ExitStack() as esw:
                    w3_sb = sb(esw, f"w3_sb{bt}", [128, 32 * 1024], BF16)
                    ctx.block_barrier()
                    m_w3 = ctx.op("gpsimd", lambda e: e.dma_start(
                        out=w3_sb[:, :].rearrange("p (a t) -> p a t", a=32),
                        in_=st.w3T.ap().rearrange("(a p) t -> p a t", p=128)), GD, 16)
                    with contextlib.ExitStack() as esp:
                        w3_ps = tuple(esp.enter_context(
                            nc.psum_tensor(f"w3_ps{bt}_{i}", [128, 1024], F32)) for i in range(2))
                        w3_rd = [None, None]
                        m_out_wr = [None, None]
                        m_sq_prev = None
                        for qt in range(8):
                            buf = qt % 2
                            ctx.waits("tensor", m_w3, w3_rd[buf])
                            if qt == 0:
                                ctx.waits("tensor", *g2_marks)
                            mm = None
                            for half in range(2):
                                for a in range(32):
                                    f = lambda e, half=half, a=a, buf=buf, qt=qt: e.matmul(
                                        w3_ps[buf][:, half * 512:(half + 1) * 512],
                                        ha_sb[:, a * 1024 + qt * 128:a * 1024 + (qt + 1) * 128],
                                        w3_sb[:, a * 1024 + half * 512:a * 1024 + (half + 1) * 512],
                                        start=(a == 0), stop=(a == 31),
                                        skip_group_check=True)
                                    last = (half == 1 and a == 31)
                                    mm = ctx.op("tensor", f, TE) if last else ctx.op_("tensor", f)
                            ctx.waits("vector", mm, m_sq_prev)
                            ctx.waits_total("vector", SD)
                            for half in range(2):
                                ctx.op_("vector", lambda e, half=half, buf=buf: e.tensor_mul(
                                    yb_sb[:, half * 512:(half + 1) * 512],
                                    w3_ps[buf][:, half * 512:(half + 1) * 512],
                                    st.s3_sb[:, half * 512:(half + 1) * 512]))
                            m_y = ctx.op("vector", lambda e, buf=buf: e.scalar_tensor_tensor(
                                yb2_sb[:, :], yb_sb[:, :], 1.0,
                                x1r_sb[:, buf * 1024:(buf + 1) * 1024],
                                ALU.mult, ALU.add, accum_out=st.stats[:, 8:9]), DV)
                            w3_rd[buf] = m_y
                            ctx.waits("scalar", m_y)
                            m_sq = ctx.op("scalar", lambda e: e.activation(
                                yb_sb[:, :], yb2_sb[:, :], AF.Square,
                                accum_out=st.stats[:, 9:10]), AC)
                            m_sq_prev = m_sq
                            if qt + 2 < 8:
                                ctx.waits("sync", m_y)
                                m_x1r[buf] = ctx.op("sync",
                                    lambda e, bt=bt, qt=qt, buf=buf: e.dma_start(
                                        out=x1r_sb[:, buf * 1024:(buf + 1) * 1024],
                                        in_=st.x1_dram.ap()[bt * BT + (qt + 2) * 128:
                                                            bt * BT + (qt + 3) * 128, :]), SD, 16)
                            else:
                                m_x1r[buf] = None
                            ctx.waits("vector", m_out_wr[buf])
                            m_x2 = _ln_chain(ctx, st, 8, yb2_sb,
                                             x2_sb[:, buf * 1024:(buf + 1) * 1024],
                                             TE, DV, AC, m_sq, m_y)
                            ctx.waits("sync", m_x2)
                            m_out_wr[buf] = ctx.op("sync",
                                lambda e, bt=bt, qt=qt, buf=buf: e.dma_start(
                                    out=st.out.ap()[bt * BT + qt * 128:bt * BT + (qt + 1) * 128, :],
                                    in_=x2_sb[:, buf * 1024:(buf + 1) * 1024]), SD, 16)
                            out_marks.append(m_out_wr[buf])
                        if bt == 1:
                            if DEBUG:
                                ctx.op("sync", lambda e: e.dma_start(
                                    out=st.dbg_x1.ap(), in_=st.x1_dram.ap()), SD, 16)
                            ctx.waits("sync", out_marks[-1])
                        ctx.run()
    return nc


def _blkT(wT, ko_n, g_n):
    Din, Dout = wT.shape
    assert Din == g_n * 128 and Dout == ko_n * 128
    return np.ascontiguousarray(wT.reshape(g_n, 128, ko_n, 128).transpose(2, 1, 0, 3))


def kernel(L0_attn_x, L0_attn_Wq, L0_attn_Wk, L0_attn_Wv, L0_attn_Wo, L0_attn_bo,
           L0_ffn_w1_w, L0_ffn_w1_s, L0_ffn_w2_w, L0_ffn_w2_s,
           L0_ffn_h1_w, L0_ffn_h1_s, L0_ffn_h2_w, L0_ffn_h2_s,
           L0_ffn_w3_w, L0_ffn_w3_s):
    f32 = lambda a: np.ascontiguousarray(np.asarray(a, dtype=np.float32))
    x = f32(L0_attn_x)
    Wq, Wk, Wv, Wo = f32(L0_attn_Wq), f32(L0_attn_Wk), f32(L0_attn_Wv), f32(L0_attn_Wo)
    bo = f32(L0_attn_bo)
    w1, w2 = f32(L0_ffn_w1_w), f32(L0_ffn_w2_w)
    h1, h2 = f32(L0_ffn_h1_w), f32(L0_ffn_h2_w)
    w3 = f32(L0_ffn_w3_w)
    s1, s2 = f32(L0_ffn_w1_s), f32(L0_ffn_w2_s)
    sh1, sh2 = f32(L0_ffn_h1_s), f32(L0_ffn_h2_s)
    s3 = f32(L0_ffn_w3_s)

    pidx = np.arange(128)
    hm_cat = np.zeros((128, 194), np.float32)
    for ci in range(8):
        for hh in range(16):
            hm_cat[:, ci * 16 + hh] = (hh == 2 * ci + pidx // 64)
    for di in range(64):
        hm_cat[di, 128 + di] = 1.0
        hm_cat[64 + di, 128 + di] = 1.0
    hm_cat[0:16, 192] = 1.0
    hm_cat[64:80, 193] = 1.0
    hm_sel = np.zeros((128, 1024), np.float32)
    for ci in range(8):
        for hh in range(16):
            hm_sel[hh, ci * 128:(ci + 1) * 128] = (hh == 2 * ci + pidx // 64)
            hm_sel[64 + hh, ci * 128:(ci + 1) * 128] = (hh == 2 * ci + pidx // 64)

    shared = {
        "wk_blk": _blkT(Wk.T, 8, 8),
        "wv_blk": _blkT(Wv.T, 8, 8),
        "hm_cat": hm_cat.astype(BFDT),
        "hm_sel": hm_sel.astype(BFDT),
        "woT": np.ascontiguousarray(Wo.T),
        "bo_bc": np.ascontiguousarray(np.broadcast_to(bo, (128, D))),
        "w1_blk": _blkT(w1.T, 32, 8),
        "w2_blk": _blkT(w2.T, 32, 8),
        "h1_blk": _blkT(h1.T, 32, 32),
        "h2_blk": _blkT(h2.T, 32, 32),
        "w3T": np.ascontiguousarray(w3.T),
        "s1_pp": np.ascontiguousarray(s1.reshape(32, 128).T),
        "s2_pp": np.ascontiguousarray(s2.reshape(32, 128).T),
        "sh1_pp": np.ascontiguousarray(sh1.reshape(32, 128).T),
        "sh2_pp": np.ascontiguousarray(sh2.reshape(32, 128).T),
        "s3_bc": np.ascontiguousarray(np.broadcast_to(s3, (128, D))),
        "ident": np.concatenate(
            [np.eye(128, dtype=np.float32),
             np.concatenate([np.eye(64, dtype=np.float32)] * 2, axis=0)],
            axis=1).astype(BFDT),
    }
    xT = [np.ascontiguousarray(x[n].T) for n in range(BATCH)]
    WqT = Wq.T
    wq_half = []
    for h in range(2):
        wh = WqT[:, h * 512:(h + 1) * 512]
        dup = np.empty((D, 1024), np.float32)
        for hl in range(8):
            blk = wh[:, hl * 64:(hl + 1) * 64]
            dup[:, hl * 128:hl * 128 + 64] = blk
            dup[:, hl * 128 + 64:(hl + 1) * 128] = blk
        wq_half.append(_blkT(np.ascontiguousarray(dup), 8, 8))
    in_maps = []
    for c in range(N_CORES):
        n, half = c // 2, c % 2
        m = dict(shared)
        m["xT_nb"] = xT[n]
        m["wq_blk"] = wq_half[half]
        m["x_own"] = np.ascontiguousarray(x[n, half * TOWN:(half + 1) * TOWN, :])
        in_maps.append(m)

    nc = build_nc()
    res = run_bass_kernel_spmd(nc, in_maps, core_ids=list(range(N_CORES)))
    outv = np.empty((BATCH, S, D), np.float32)
    for c in range(N_CORES):
        n, half = c // 2, c % 2
        outv[n, half * TOWN:(half + 1) * TOWN, :] = res.results[c]["out"]
    return outv
